# version 05
# baseline (speedup 1.0000x reference)
import contextlib
import numpy as np
import concourse.bass as bass
import concourse.mybir as mybir
from concourse.bass_utils import run_bass_kernel_spmd

F32 = mybir.dt.float32
BF16 = mybir.dt.bfloat16
F8 = mybir.dt.float8e4
AF = mybir.ActivationFunctionType
ALU = mybir.AluOpType
AX = mybir.AxisListType.X

D = 1024
S = 4096
HID = 4096
H = 16
HD = 64
P = 512
NP = S // P
DC = D // 128
HC = HID // 128
EPS = 1e-5
NCORES = 8
WRN = 8

ENGINES = ("sync", "scalar", "vector", "tensor", "gpsimd")


class Sem:
    __slots__ = ("h", "n")
    def __init__(self, handle):
        self.h = handle
        self.n = 0


class Prog:
    def __init__(self):
        self.q = {e: [] for e in ENGINES}

    def emit(self, eng, fn, sem=None, amt=1):
        if sem is not None:
            sem.n += amt
            self.q[eng].append(lambda e, fn=fn, h=sem.h, a=amt: fn(e).then_inc(h, a))
        else:
            self.q[eng].append(fn)
        return sem.n if sem is not None else None

    def dma(self, eng, sem, out, in_):
        sem.n += 16
        self.q[eng].append(
            lambda e, o=out, i=in_, h=sem.h: e.dma_start(out=o, in_=i).then_inc(h, 16))
        return sem.n

    def dma_t(self, eng, sem, out, in_):
        sem.n += 16
        self.q[eng].append(
            lambda e, o=out, i=in_, h=sem.h: e.dma_start(out=o, in_=i, transpose=True).then_inc(h, 16))
        return sem.n

    def wait(self, eng, sem, n=None):
        tgt = sem.n if n is None else n
        if tgt > 0:
            self.q[eng].append(lambda e, h=sem.h, t=tgt: e.wait_ge(h, t))

    def flush(self, nc):
        with nc.Block() as block:
            for name in ENGINES:
                ops = self.q[name]
                if ops:
                    getattr(block, name)(lambda e, ops=ops: [f(e) for f in ops])
        self.q = {e: [] for e in ENGINES}


class Arena:
    def __init__(self, t, total):
        self.t = t
        self.off = 0
        self.total = total

    def alloc(self, n, align=16):
        o = (self.off + align - 1) // align * align
        self.off = o + n
        assert self.off <= self.total, f"arena overflow {self.off} > {self.total}"
        return o


def build(upto=5, debug=False):
    nc = bass.Bass("TRN2", target_bir_lowering=False)

    x0 = nc.dram_tensor("x", [D, S], F32, kind="ExternalInput").ap()
    FFN_MATS = (("w1", HC, DC), ("w2", HC, DC), ("h1", HC, HC), ("h2", HC, HC), ("w3", DC, HC))
    wslab = {nm: nc.dram_tensor(nm + "s", [oc, 128, dcm * 128], F32, kind="ExternalInput").ap()
             for nm, oc, dcm in FFN_MATS}
    scols = {nm: nc.dram_tensor(nm, [128, w], F32, kind="ExternalInput").ap()
             for nm, w in (("s1c", HC), ("s2c", HC), ("s1hc", HC), ("s2hc", HC), ("s3c", DC))}
    gen_in = {}
    GEN_MATS = (("ca", 24, DC), ("cp", DC, DC), ("cf", HC, DC), ("cm", DC, HC))
    for nm, oc, dcm in GEN_MATS:
        for part in ("sub0", "sub1", "Ww", "Bw"):
            gen_in[f"{nm}_{part}"] = nc.dram_tensor(
                f"{nm}_{part}", [oc, 128, dcm * 128], F32, kind="ExternalInput").ap()
        gen_in[f"{nm}_wlw"] = nc.dram_tensor(f"{nm}_wlw", [128, 2], F32, kind="ExternalInput").ap()
        for part in ("bsub0", "bsub1", "bWw", "bBw"):
            gen_in[f"{nm}_{part}"] = nc.dram_tensor(
                f"{nm}_{part}", [128, oc], F32, kind="ExternalInput").ap()
        gen_in[f"{nm}_wlb"] = nc.dram_tensor(f"{nm}_wlb", [128, 2], F32, kind="ExternalInput").ap()
    lncols = {nm: nc.dram_tensor(nm, [128, DC], F32, kind="ExternalInput").ap()
              for nm in ("ln1g", "ln1b", "ln2g", "ln2b")}
    identc = nc.dram_tensor("identc", [128, 128], F32, kind="ExternalInput").ap()
    ones128c = nc.dram_tensor("ones128", [128, 128], F32, kind="ExternalInput").ap()
    maskc = nc.dram_tensor("maskc", [128, 2048], F32, kind="ExternalInput").ap()
    epsrowc = nc.dram_tensor("epsrow", [1, 1], F32, kind="ExternalInput").ap()

    out = nc.dram_tensor("out", [D, S], F32, kind="ExternalOutput").ap()
    sx1 = nc.dram_tensor("sx1", [D, S], F32).ap()
    shs = nc.dram_tensor("shs", [D, S], F32).ap()
    sx2 = nc.dram_tensor("sx2", [D, S], F32).ap()
    sqh = nc.dram_tensor("sqh", [D, S], BF16).ap()
    sql = nc.dram_tensor("sql", [D, S], BF16).ap()
    skh = nc.dram_tensor("skh", [D, S], BF16).ap()
    skl = nc.dram_tensor("skl", [D, S], BF16).ap()
    svh = nc.dram_tensor("svh", [S, D], BF16).ap()
    svl = nc.dram_tensor("svl", [S, D], BF16).ap()
    so = nc.dram_tensor("so", [D, S], F32).ap()
    wgen = {"ca": nc.dram_tensor("wca", [24, 128, D], F32).ap(),
            "cp": nc.dram_tensor("wcp", [DC, 128, D], F32).ap(),
            "cf": nc.dram_tensor("wcf", [HC, 128, D], F32).ap(),
            "cm": nc.dram_tensor("wcm", [DC, 128, HID], F32).ap()}
    wf8 = {nm: nc.dram_tensor(nm + "f8", [oc, 128, dcm * 128], F8).ap()
           for nm, oc, dcm in FFN_MATS}

    dbg = {}
    if debug is True:
        for nm, shp in (("dhs", [D, S]), ("do", [D, S]), ("dx2", [D, S])):
            dbg[nm] = nc.dram_tensor(nm, shp, F32, kind="ExternalOutput").ap()

    with contextlib.ExitStack() as _stk:
        GN = 8192
        garena = _stk.enter_context(nc.sbuf_tensor("garena", [128, GN], F32))
        ps = _stk.enter_context(nc.psum_tensor("ps", [128, 8, 512], F32))

        def _sem(name):
            return Sem(_stk.enter_context(nc.semaphore(name)))
        LD, MM, EV, E2, ST, X1, X2, KV = (
            _sem(n) for n in ("LD", "MM", "EV", "E2", "ST", "X1", "X2", "KV"))
        WS = [_sem(f"WS{i}") for i in range(WRN)]
        QS = [_sem(f"QS{i}") for i in range(2)]
        GS = [_sem(f"GS{i}") for i in range(2)]
        GW = [_sem(f"GW{i}") for i in range(2)]
        SS = [_sem(f"SS{i}") for i in range(4)]
        A = Arena(garena, GN)
        Pr = Prog()

        def V(o, n, p0=0, p1=128):
            return garena[p0:p1, o:o + n]

        o_ident = A.alloc(128)
        o_ones128 = A.alloc(128)
        o_mask = A.alloc(2048)
        o_eps = A.alloc(16)
        o_z4 = A.alloc(512)
        o_z5 = A.alloc(512)
        o_sc = {nm: A.alloc(w) for nm, w in
                (("s1c", HC), ("s2c", HC), ("s1hc", HC), ("s2hc", HC), ("s3c", DC), ("s3hc", DC))}
        o_ln = {nm: A.alloc(DC) for nm in ("ln1g", "ln1b", "ln2g", "ln2b")}
        o_bias = {"ca": A.alloc(24), "cp": A.alloc(DC), "cf": A.alloc(HC), "cm": A.alloc(DC)}
        o_bk8 = A.alloc(8)
        o_rows = A.alloc(6 * 512)

        ident = V(o_ident, 128)
        ones128v = V(o_ones128, 128)
        epsv = V(o_eps, 1, 0, 1)

        def row(i, n=512):
            return V(o_rows + i * 512, n, 0, 1)

        def scol(name, i):
            return V(o_sc[name] + i, 1)

        def lncol(name, i):
            return V(o_ln[name] + i, 1)

        def bcol(name, i):
            return V(o_bias[name] + i, 1)

        bank_free = [(EV, 0)] * 8
        ln_state = {"bc_mm": 0}

        def bank_wait(eng, b):
            sem, n = bank_free[b]
            Pr.wait(eng, sem, n)

        def drain_stores():
            Pr.wait("gpsimd", ST)
            for s_ in SS:
                Pr.wait("gpsimd", s_)

        Pr.dma("sync", LD, ident, identc)
        Pr.dma("sync", LD, V(o_mask, 2048), maskc)
        Pr.dma("sync", LD, ones128v, ones128c)
        Pr.dma("sync", LD, epsv, epsrowc)
        for nm in ("s1c", "s2c", "s1hc", "s2hc", "s3c"):
            Pr.dma("sync", LD, V(o_sc[nm], scols[nm].shape[-1]), scols[nm])
        for nm in ("ln1g", "ln1b", "ln2g", "ln2b"):
            Pr.dma("sync", LD, V(o_ln[nm], DC), lncols[nm])
        Pr.wait("vector", LD)
        Pr.emit("vector", lambda e, o1=V(o_sc["s3hc"], DC), o2=V(o_sc["s3c"], DC):
                e.tensor_scalar_mul(o1, o2, 0.5), EV)
        Pr.emit("vector", lambda e, a=V(o_z4, 512): e.memset(a, 0.0))
        Pr.emit("vector", lambda e, a=V(o_z5, 512): e.memset(a, 0.0))
        Pr.flush(nc)

        with (
            nc.sbuf_tensor("p0f", [128, 14592], F32) as t0f,
            nc.sbuf_tensor("p0e", [128, 2048], F8) as t0e,
        ):
            A0 = Arena(t0f, 14592)
            o_wl = A0.alloc(16)
            o_gs0 = [A0.alloc(1024) for _ in range(2)]
            o_gs1 = [A0.alloc(1024) for _ in range(2)]
            o_gww = [A0.alloc(1024) for _ in range(2)]
            o_gbw = [A0.alloc(1024) for _ in range(2)]
            o_gt = [A0.alloc(1024) for _ in range(2)]
            o_gw32 = [A0.alloc(1024) for _ in range(2)]
            o_bt = [A0.alloc(64) for _ in range(4)]

            def V0(o, n, p0=0, p1=128):
                return t0f[p0:p1, o:o + n]

            for nm, oc, dcm in GEN_MATS:
                Pr.wait("sync", EV)
                Pr.dma("sync", LD, V0(o_wl, 2), gen_in[f"{nm}_wlw"])
                npc = dcm * 128 // 1024
                jglob = 0
                for o in range(oc):
                    for pc in range(npc):
                        r = jglob % 2
                        sl = slice(pc * 1024, (pc + 1) * 1024)
                        if jglob >= 2:
                            Pr.wait("sync", SS[r])
                        Pr.dma("sync", GS[r], V0(o_gs0[r], 1024), gen_in[f"{nm}_sub0"][o, :, sl])
                        Pr.dma("sync", GS[r], V0(o_gs1[r], 1024), gen_in[f"{nm}_sub1"][o, :, sl])
                        Pr.dma("sync", GW[r], V0(o_gww[r], 1024), gen_in[f"{nm}_Ww"][o, :, sl])
                        Pr.dma("sync", GW[r], V0(o_gbw[r], 1024), gen_in[f"{nm}_Bw"][o, :, sl])
                        Pr.wait("vector", GS[r])
                        Pr.emit("vector", lambda e, a=V0(o_gt[r], 1024), b=V0(o_gs0[r], 1024),
                                w=V0(o_wl, 1): e.tensor_scalar_mul(a, b, w))
                        Pr.emit("vector", lambda e, a=V0(o_gs0[r], 1024), b=V0(o_gs1[r], 1024),
                                w=V0(o_wl + 1, 1), c=V0(o_gt[r], 1024):
                                e.scalar_tensor_tensor(a, b, w, c, op0=ALU.mult, op1=ALU.add), X1)
                        Pr.wait("scalar", X1)
                        Pr.emit("scalar", lambda e, a=V0(o_gt[r], 1024), b=V0(o_gs0[r], 1024):
                                e.activation(a, b, AF.Tanh), X2)
                        Pr.wait("vector", X2)
                        Pr.wait("vector", GW[r])
                        Pr.emit("vector", lambda e, a=V0(o_gs1[r], 1024), b=V0(o_gt[r], 1024),
                                c=V0(o_gww[r], 1024): e.tensor_mul(a, b, c))
                        Pr.emit("vector", lambda e, a=V0(o_gw32[r], 1024), b=V0(o_gs1[r], 1024),
                                c=V0(o_gbw[r], 1024): e.tensor_add(a, b, c), EV)
                        Pr.wait("gpsimd", EV)
                        Pr.dma("gpsimd", SS[r], wgen[nm][o, :, sl], V0(o_gw32[r], 1024))
                        jglob += 1

            Pr.wait("sync", EV)
            cast_jobs = [(nm, o, pc) for nm, oc, dcm in FFN_MATS
                         for o in range(oc) for pc in range(dcm // 8)]
            for jg, (nm, o, pc) in enumerate(cast_jobs):
                r = jg % 2
                sl = slice(pc * 1024, (pc + 1) * 1024)
                if jg >= 2:
                    Pr.wait("sync", SS[2 + r])
                Pr.dma("sync", GS[r], V0(o_gs0[r], 1024), wslab[nm][o, :, sl])
                Pr.wait("vector", GS[r])
                Pr.emit("vector", lambda e, a=t0e[:, r * 1024:(r + 1) * 1024],
                        b=V0(o_gs0[r], 1024): e.tensor_copy(a, b), EV)
                Pr.wait("gpsimd", EV)
                Pr.dma("gpsimd", SS[2 + r], wf8[nm][o, :, sl], t0e[:, r * 1024:(r + 1) * 1024])

            for nm, oc, dcm in GEN_MATS:
                Pr.wait("sync", EV)
                Pr.dma("sync", LD, V0(o_wl, 2), gen_in[f"{nm}_wlb"])
                Pr.dma("sync", LD, V0(o_bt[0], oc), gen_in[f"{nm}_bsub0"])
                Pr.dma("sync", LD, V0(o_bt[1], oc), gen_in[f"{nm}_bsub1"])
                Pr.dma("sync", LD, V0(o_bt[2], oc), gen_in[f"{nm}_bWw"])
                Pr.dma("sync", LD, V0(o_bt[3], oc), gen_in[f"{nm}_bBw"])
                Pr.wait("vector", LD)
                ob = V(o_bias[nm], oc)
                Pr.emit("vector", lambda e, a=ob, b=V0(o_bt[0], oc), w=V0(o_wl, 1):
                        e.tensor_scalar_mul(a, b, w))
                Pr.emit("vector", lambda e, a=ob, b=V0(o_bt[1], oc), w=V0(o_wl + 1, 1):
                        e.scalar_tensor_tensor(a, b, w, a, op0=ALU.mult, op1=ALU.add), X1)
                Pr.wait("scalar", X1)
                Pr.emit("scalar", lambda e, a=ob: e.activation(a, a, AF.Tanh), X2)
                Pr.wait("vector", X2)
                Pr.emit("vector", lambda e, a=ob, b=V0(o_bt[2], oc): e.tensor_mul(a, a, b))
                Pr.emit("vector", lambda e, a=ob, b=V0(o_bt[3], oc): e.tensor_add(a, a, b), EV)
            Pr.emit("vector", lambda e, a=V(o_bk8, 8), b=V(o_bias["ca"] + 8, 8):
                    e.tensor_scalar_mul(a, b, 0.125), EV)
            drain_stores()
            Pr.flush(nc)

        def emit_ln(srcf, dstf, sqf, gname=None, width=P):
            Pr.wait("scalar", EV)
            for c in range(DC):
                Pr.emit("scalar", lambda e, a=sqf(c), b=srcf(c): e.activation(a, b, AF.Square), E2)
            bank_wait("tensor", 6)
            bank_wait("tensor", 7)
            for c in range(DC):
                Pr.wait("tensor", E2)
                Pr.emit("tensor", lambda e, r=srcf(c), c=c:
                        e.matmul(ps[:, 6, 0:width], ones128v, r,
                                 start=(c == 0), stop=(c == DC - 1)),
                        MM if c == DC - 1 else None)
                Pr.emit("tensor", lambda e, r=sqf(c), c=c:
                        e.matmul(ps[:, 7, 0:width], ones128v, r,
                                 start=(c == 0), stop=(c == DC - 1)),
                        MM if c == DC - 1 else None)
            Pr.wait("vector", MM)
            Pr.emit("vector", lambda e, a=row(0, width): e.tensor_scalar_mul(a, ps[0:1, 6, 0:width], 1.0 / D))
            Pr.emit("vector", lambda e, a=row(1, width): e.tensor_scalar_mul(a, ps[0:1, 7, 0:width], 1.0 / D), X1)
            Pr.wait("vector", X1)
            Pr.emit("vector", lambda e, a=row(2, width), m=row(0, width): e.tensor_mul(a, m, m), X1)
            Pr.wait("vector", X1)
            Pr.emit("vector", lambda e, a=row(2, width), b=row(1, width), c=row(2, width):
                    e.tensor_sub(a, b, c), X1)
            bank_free[6] = (X1, X1.n)
            bank_free[7] = (X1, X1.n)
            Pr.wait("scalar", X1)
            Pr.emit("scalar", lambda e, a=row(3, width), b=row(2, width):
                    e.activation(a, b, AF.Sqrt, bias=epsv, scale=1.0), X2)
            Pr.wait("vector", X2)
            Pr.emit("vector", lambda e, a=row(4, width), b=row(3, width): e.reciprocal(a, b), X1)
            Pr.wait("vector", X1)
            Pr.emit("vector", lambda e, a=row(5, width), m=row(0, width), r=row(4, width):
                    e.tensor_mul(a, m, r), X1)
            Pr.wait("vector", X1)
            Pr.emit("vector", lambda e, a=row(5, width): e.tensor_scalar_mul(a, a, -1.0), X1)
            Pr.wait("vector", X1)
            Pr.wait("vector", MM, ln_state["bc_mm"])
            Pr.emit("vector", lambda e, a=V(o_z4, width, 0, 1), b=row(4, width): e.tensor_copy(a, b))
            Pr.emit("vector", lambda e, a=V(o_z5, width, 0, 1), b=row(5, width): e.tensor_copy(a, b), X1)
            Pr.wait("tensor", X1)
            bank_wait("tensor", 4)
            bank_wait("tensor", 5)
            Pr.emit("tensor", lambda e, r=V(o_z4, width): e.matmul(
                ps[:, 4, 0:width], ones128v, r, start=True, stop=True))
            Pr.emit("tensor", lambda e, r=V(o_z5, width): e.matmul(
                ps[:, 5, 0:width], ones128v, r, start=True, stop=True), MM)
            ln_state["bc_mm"] = MM.n
            Pr.wait("vector", MM)
            for c in range(DC):
                last = (c == DC - 1)
                Pr.emit("vector", lambda e, a=dstf(c), b=srcf(c): e.tensor_mul(a, b, ps[:, 4, 0:width]))
                if gname is None:
                    Pr.emit("vector", lambda e, a=dstf(c):
                            e.tensor_add(a, a, ps[:, 5, 0:width]), EV if last else None)
                else:
                    Pr.emit("vector", lambda e, a=dstf(c):
                            e.tensor_add(a, a, ps[:, 5, 0:width]))
                    Pr.emit("vector", lambda e, a=dstf(c),
                            g=lncol(gname + "g", c), b=lncol(gname + "b", c):
                            e.tensor_scalar(a, a, g, b, op0=ALU.mult, op1=ALU.add),
                            EV if last else None)
            bank_free[4] = (EV, EV.n)
            bank_free[5] = (EV, EV.n)

        def emit_lin(wdram, oc_n, dcm, rhsf, wringf, epilogue, piece_free,
                     banks=(0, 1, 2, 3), rhsf2=None, ring_n=WRN, ws=None):
            if ws is None:
                ws = WS
            ppo = dcm // 8
            nb = len(banks)
            for o in range(oc_n):
                j0 = len(piece_free)
                for q in range(ppo):
                    j = j0 + q
                    if j >= ring_n:
                        Pr.wait("sync", MM, piece_free[j - ring_n])
                    Pr.dma("sync", ws[j % ring_n], wringf(j % ring_n, slice(0, 1024)),
                           wdram[o, :, q * 1024:(q + 1) * 1024])
                b = banks[o % nb]
                for q in range(ppo):
                    Pr.wait("tensor", ws[(j0 + q) % ring_n])
                bank_wait("tensor", b)
                npass = 2 if rhsf2 is not None else 1
                for c in range(dcm):
                    jj = (j0 + c // 8) % ring_n
                    wv = wringf(jj, slice((c % 8) * 128, (c % 8) * 128 + 128))
                    for pss in range(npass):
                        rv = rhsf(c) if pss == 0 else rhsf2(c)
                        first = (c == 0 and pss == 0)
                        lastm = (c == dcm - 1 and pss == npass - 1)
                        Pr.emit("tensor", lambda e, w=wv, r=rv, b=b, f=first, l=lastm:
                                e.matmul(ps[:, b, :], w, r, start=f, stop=l),
                                MM if lastm else None)
                for _ in range(ppo):
                    piece_free.append(MM.n)
                epilogue(o, b)

        ffn_id = [0]

        def emit_ffn(x_in_dram, x_out_dram):
            fid = ffn_id[0]; ffn_id[0] += 1
            with (
                nc.sbuf_tensor(f"ffa{fid}", [128, 8704], F32) as tfa,
                nc.sbuf_tensor(f"ffb{fid}", [128, 65536], BF16) as tfb,
                nc.sbuf_tensor(f"ff8{fid}", [128, WRN * 1024], F8) as tf8,
            ):
                o_x = 0
                o_sh = DC * P
                FX = lambda c: tfa[:, o_x + c * P: o_x + (c + 1) * P]
                SH = lambda off, n=P: tfa[:, o_sh + off: o_sh + off + n]

                def HAh(c): return tfb[:, c * P:(c + 1) * P]
                def HAl(c): return tfb[:, 16384 + c * P: 16384 + (c + 1) * P]
                def HBh(c): return tfb[:, 32768 + c * P: 32768 + (c + 1) * P]
                def HBl(c): return tfb[:, 49152 + c * P: 49152 + (c + 1) * P]
                XNh = HBh
                def XNl(c): return tfb[:, 32768 + DC * P + c * P: 32768 + (DC + c + 1) * P]
                WR = lambda slot, sl: tf8[:, slot * 1024:(slot + 1) * 1024][:, sl]

                ev_prev_x = [0] * DC
                piece_free = []
                for p in range(NP):
                    tsl = slice(p * P, (p + 1) * P)
                    ldx = []
                    for c in range(DC):
                        Pr.wait("sync", EV, ev_prev_x[c])
                        ldx.append(Pr.dma("sync", LD, FX(c),
                                          x_in_dram[c * 128:(c + 1) * 128, tsl]))
                    Pr.wait("scalar", LD, ldx[-1])
                    Pr.wait("vector", LD, ldx[-1])
                    Pr.wait("tensor", LD, ldx[-1])
                    for i in range(4):
                        Pr.wait("vector", SS[i])
                        Pr.wait("scalar", SS[i])
                    Pr.wait("vector", MM)
                    Pr.wait("scalar", MM)
                    emit_ln(FX, lambda c: SH(c * 512), lambda c: SH(c * 512))
                    for c in range(DC):
                        Pr.emit("scalar", lambda e, a=XNh(c), b=SH(c * 512):
                                e.activation(a, b, AF.Copy), E2)
                        Pr.wait("vector", E2)
                        Pr.emit("vector", lambda e, a=XNl(c), b=SH(c * 512), h=XNh(c):
                                e.tensor_sub(a, b, h), EV)
                    split_ev = EV.n
                    t1_x2 = [0, 0]
                    g32_ev = [0, 0]
                    Pr.wait("tensor", EV, split_ev)
                    for o in range(HC):
                        for mat, wd in ((0, wf8["w1"]), (1, wf8["w2"])):
                            j = len(piece_free)
                            if j >= WRN:
                                Pr.wait("sync", MM, piece_free[j - WRN])
                            Pr.dma("sync", WS[j % WRN], WR(j % WRN, slice(0, 1024)), wd[o, :, :])
                            b = (o % 2) * 2 + mat
                            Pr.wait("tensor", WS[j % WRN])
                            bank_wait("tensor", b)
                            for c in range(DC):
                                for pss in range(2):
                                    rv = XNh(c) if pss == 0 else XNl(c)
                                    Pr.emit("tensor", lambda e,
                                            w=WR(j % WRN, slice(c * 128, c * 128 + 128)),
                                            r=rv, b=b, f=(c == 0 and pss == 0),
                                            l=(c == DC - 1 and pss == 1):
                                            e.matmul(ps[:, b, :], w, r, start=f, stop=l),
                                            MM if (c == DC - 1 and pss == 1) else None)
                            piece_free.append(MM.n)
                            if mat == 0:
                                mm1 = MM.n
                            else:
                                mm2 = MM.n
                        b1 = (o % 2) * 2
                        b2 = b1 + 1
                        rt = o % 2
                        Pr.wait("scalar", MM, mm1)
                        Pr.wait("scalar", X2, t1_x2[rt])
                        Pr.emit("scalar", lambda e, a=SH(rt * 512), b=b1, s=scol("s1c", o):
                                e.activation(a, ps[:, b, :], AF.Silu, scale=s), E2)
                        bank_free[b1] = (E2, E2.n)
                        Pr.wait("vector", E2)
                        Pr.wait("vector", MM, mm2)
                        Pr.wait("vector", EV, g32_ev[rt])
                        Pr.emit("vector", lambda e, a=SH(1024 + rt * 512), b=b2, s=scol("s2c", o),
                                t=SH(rt * 512):
                                e.scalar_tensor_tensor(a, ps[:, b, :], s, t,
                                                       op0=ALU.mult, op1=ALU.mult), X2)
                        t1_x2[rt] = X2.n
                        bank_free[b2] = (X2, X2.n)
                        Pr.wait("scalar", X2)
                        Pr.emit("scalar", lambda e, a=HAh(o), b=SH(1024 + rt * 512):
                                e.activation(a, b, AF.Copy), E2)
                        Pr.wait("vector", E2)
                        Pr.emit("vector", lambda e, a=HAl(o), b=SH(1024 + rt * 512), h=HAh(o):
                                e.tensor_sub(a, b, h), EV)
                        g32_ev[rt] = EV.n
                    Pr.wait("tensor", EV)

                    def ep_h(dsth, dstl, sname):
                        st = {"g": [0, 0]}
                        def ep(o, b):
                            rt = o % 2
                            Pr.wait("scalar", MM)
                            Pr.wait("scalar", EV, st["g"][rt])
                            Pr.emit("scalar", lambda e, a=SH(1024 + rt * 512), b=b, s=scol(sname, o):
                                    e.activation(a, ps[:, b, :], AF.Gelu, scale=s), X2)
                            bank_free[b] = (X2, X2.n)
                            Pr.wait("scalar", X2)
                            Pr.emit("scalar", lambda e, a=dsth(o), b=SH(1024 + rt * 512):
                                    e.activation(a, b, AF.Copy), E2)
                            Pr.wait("vector", E2)
                            Pr.emit("vector", lambda e, a=dstl(o), b=SH(1024 + rt * 512), h=dsth(o):
                                    e.tensor_sub(a, b, h), EV)
                            st["g"][rt] = EV.n
                        return ep

                    emit_lin(wf8["h1"], HC, HC, HAh, WR, ep_h(HBh, HBl, "s1hc"),
                             piece_free, rhsf2=HAl)
                    Pr.wait("tensor", EV)
                    emit_lin(wf8["h2"], HC, HC, HBh, WR, ep_h(HAh, HAl, "s2hc"),
                             piece_free, rhsf2=HBl)
                    Pr.wait("tensor", EV)
                    ev_cur_x = [0] * DC

                    def ep_w3(o, b):
                        sslot = o % 4
                        Pr.wait("vector", MM)
                        Pr.wait("vector", SS[sslot])
                        Pr.emit("vector", lambda e, a=SH(2048 + sslot * 512), b=b,
                                s=scol("s3hc", o), x=FX(o):
                                e.scalar_tensor_tensor(a, ps[:, b, :], s, x,
                                                       op0=ALU.mult, op1=ALU.add), EV)
                        bank_free[b] = (EV, EV.n)
                        ev_cur_x[o] = EV.n
                        Pr.wait("gpsimd", EV)
                        Pr.dma("gpsimd", SS[sslot], x_out_dram[o * 128:(o + 1) * 128, tsl],
                               SH(2048 + sslot * 512))

                    emit_lin(wf8["w3"], DC, HC, HAh, WR, ep_w3, piece_free, rhsf2=HAl)
                    ev_prev_x = ev_cur_x
                drain_stores()
                Pr.flush(nc)

        def emit_p2():
            with (
                nc.sbuf_tensor("p2f", [128, 23040], F32) as t2,
                nc.sbuf_tensor("p2b", [128, 4096], BF16) as t2b,
            ):
                a = Arena(t2, 23040)
                o_x = a.alloc(DC * P)
                o_hs = a.alloc(DC * P)
                o_h = a.alloc(DC * P)
                o_wr = [a.alloc(1024) for _ in range(6)]
                o_vt = a.alloc(P)
                o_qo = [a.alloc(P) for _ in range(2)]
                QH = lambda rt: t2b[:, rt * 1024:rt * 1024 + 512]
                QL = lambda rt: t2b[:, rt * 1024 + 512:rt * 1024 + 1024]
                VH = lambda rv: t2b[:, 2048 + rv * 256:2048 + rv * 256 + 128]
                VL = lambda rv: t2b[:, 2048 + rv * 256 + 128:2048 + rv * 256 + 256]

                def V2(o, n, p0=0, p1=128):
                    return t2[p0:p1, o:o + n]
                FXx = lambda c: V2(o_x + c * P, P)
                FHS = lambda c: V2(o_hs + c * P, P)
                FH = lambda c: V2(o_h + c * P, P)

                ev_prev_x = [0] * DC
                piece_free = []
                qo_st = [0, 0]
                vtt_st = [0, 0]
                vt_mm = [0]
                for p in range(NP):
                    tsl = slice(p * P, (p + 1) * P)
                    ldx = []
                    for c in range(DC):
                        Pr.wait("sync", EV, ev_prev_x[c])
                        ldx.append(Pr.dma("sync", LD, FXx(c),
                                          sx1[c * 128:(c + 1) * 128, tsl]))
                    Pr.wait("scalar", LD, ldx[-1])
                    Pr.wait("vector", LD, ldx[-1])
                    Pr.wait("tensor", LD, ldx[-1])
                    Pr.wait("scalar", MM)
                    Pr.wait("scalar", ST)
                    Pr.wait("vector", MM)
                    emit_ln(FXx, FHS, FHS)
                    for c in range(DC):
                        ev_prev_x[c] = EV.n
                    Pr.wait("gpsimd", EV)
                    for c in range(DC):
                        Pr.dma("gpsimd", ST, shs[c * 128:(c + 1) * 128, tsl], FHS(c))
                        if upto == 2:
                            Pr.dma("gpsimd", ST, out[c * 128:(c + 1) * 128, tsl], FHS(c))
                        if debug is True:
                            Pr.dma("gpsimd", ST, dbg["dhs"][c * 128:(c + 1) * 128, tsl], FHS(c))
                    emit_ln(FHS, FH, FH, gname="ln1")
                    for o in range(24):
                        j = len(piece_free)
                        if j >= 6:
                            Pr.wait("sync", MM, piece_free[j - 6])
                        Pr.dma("sync", WS[j % 6], V2(o_wr[j % 6], 1024), wgen["ca"][o, :, :])
                        b = o % 4
                        Pr.wait("tensor", WS[j % 6])
                        bank_wait("tensor", b)
                        if o == 0:
                            Pr.wait("tensor", EV)
                        for c in range(DC):
                            Pr.emit("tensor", lambda e, w=V2(o_wr[j % 6] + c * 128, 128),
                                    r=FH(c), b=b, c=c:
                                    e.matmul(ps[:, b, :], w, r, start=(c == 0), stop=(c == DC - 1)),
                                    MM if c == DC - 1 else None)
                        piece_free.append(MM.n)
                        rt = o % 2
                        if o < 16:
                            Pr.wait("vector", MM)
                            Pr.wait("vector", SS[rt], qo_st[rt])
                            if o < 8:
                                Pr.emit("vector", lambda e, a=V2(o_qo[rt], P), b=b, s=bcol("ca", o):
                                        e.tensor_scalar(a, ps[:, b, :], s, None, op0=ALU.add), EV)
                                dsth, dstl, drow = sqh, sql, o
                            else:
                                Pr.emit("vector", lambda e, a=V2(o_qo[rt], P), b=b,
                                        s=V(o_bk8 + (o - 8), 1):
                                        e.tensor_scalar(a, ps[:, b, :], 0.125, s,
                                                        op0=ALU.mult, op1=ALU.add), EV)
                                dsth, dstl, drow = skh, skl, o - 8
                            bank_free[b] = (EV, EV.n)
                            Pr.wait("scalar", EV)
                            Pr.emit("scalar", lambda e, a=QH(rt), b=V2(o_qo[rt], P):
                                    e.activation(a, b, AF.Copy), E2)
                            Pr.wait("vector", E2)
                            Pr.emit("vector", lambda e, a=QL(rt), b=V2(o_qo[rt], P), h=QH(rt):
                                    e.tensor_sub(a, b, h), EV)
                            Pr.wait("gpsimd", EV)
                            Pr.dma("gpsimd", SS[rt], dsth[drow * 128:(drow + 1) * 128, tsl], QH(rt))
                            qo_st[rt] = Pr.dma("gpsimd", SS[rt],
                                               dstl[drow * 128:(drow + 1) * 128, tsl], QL(rt))
                        else:
                            Pr.wait("vector", MM)
                            Pr.wait("vector", MM, vt_mm[0])
                            Pr.emit("vector", lambda e, a=V2(o_vt, P), b=b, s=bcol("ca", o):
                                    e.tensor_scalar(a, ps[:, b, :], s, None, op0=ALU.add), EV)
                            bank_free[b] = (EV, EV.n)
                            Pr.wait("tensor", EV)
                            for jt in range(P // 128):
                                bt = 4 + (jt % 2)
                                bank_wait("tensor", bt)
                                Pr.emit("tensor", lambda e, src=V2(o_vt + jt * 128, 128), bt=bt:
                                        e.transpose(ps[:, bt, 0:128], src, ident), MM)
                                vt_mm[0] = MM.n
                                rv = jt % 2
                                Pr.wait("scalar", MM)
                                Pr.wait("scalar", SS[2 + rv], vtt_st[rv])
                                Pr.emit("scalar", lambda e, a=VH(rv), bt=bt:
                                        e.activation(a, ps[:, bt, 0:128], AF.Copy), E2)
                                Pr.wait("vector", E2)
                                Pr.emit("vector", lambda e, a=VL(rv), bt=bt, h=VH(rv):
                                        e.tensor_sub(a, ps[:, bt, 0:128], h), EV)
                                bank_free[bt] = (EV, EV.n)
                                Pr.wait("gpsimd", EV)
                                trow = p * P + jt * 128
                                Pr.dma("gpsimd", SS[2 + rv],
                                       svh[trow:trow + 128, (o - 16) * 128:(o - 15) * 128], VH(rv))
                                vtt_st[rv] = Pr.dma("gpsimd", SS[2 + rv],
                                                    svl[trow:trow + 128, (o - 16) * 128:(o - 15) * 128],
                                                    VL(rv))
                drain_stores()
                Pr.flush(nc)

        def emit_p3():
            with (
                nc.sbuf_tensor("p3f", [128, 8704], F32) as t3,
                nc.sbuf_tensor("p3b", [128, 66560], BF16) as t3b,
            ):
                a = Arena(t3, 8704)
                o_p32 = [a.alloc(S) for _ in range(2)]
                o_nm = a.alloc(16)
                o_bm = [a.alloc(16) for _ in range(2)]
                o_z = a.alloc(16)
                o_rc = [a.alloc(16) for _ in range(2)]
                o_ot = [a.alloc(128) for _ in range(2)]
                ab = Arena(t3b, 66560)
                b_kh = ab.alloc(2 * S)
                b_kl = ab.alloc(2 * S)
                b_vh = ab.alloc(HC * 256)
                b_vl = ab.alloc(HC * 256)
                b_q = [ab.alloc(256) for _ in range(2)]
                b_pt = [ab.alloc(2 * S) for _ in range(2)]
                b_ph = [ab.alloc(S) for _ in range(2)]
                b_pl = [ab.alloc(S) for _ in range(2)]

                def V3(o, n, p0=0, p1=128):
                    return t3[p0:p1, o:o + n]

                def B3(o, n, p0=0, p1=128):
                    return t3b[p0:p1, o:o + n]

                for g in range(4):
                    for c in range(2):
                        Pr.dma("sync", KV, B3(b_kh + c * S, S),
                               skh[(2 * g + c) * 128:(2 * g + c + 1) * 128, :])
                        Pr.dma("sync", KV, B3(b_kl + c * S, S),
                               skl[(2 * g + c) * 128:(2 * g + c + 1) * 128, :])
                    for kchunk in range(HC):
                        Pr.dma("sync", KV, B3(b_vh + kchunk * 256, 256),
                               svh[kchunk * 128:(kchunk + 1) * 128, 256 * g:256 * (g + 1)])
                        Pr.dma("sync", KV, B3(b_vl + kchunk * 256, 256),
                               svl[kchunk * 128:(kchunk + 1) * 128, 256 * g:256 * (g + 1)])
                    Pr.wait("tensor", KV)
                    strip_mm = [0, 0]
                    strip_e2 = [0, 0]
                    strip_ev = [0, 0]
                    q_mm = [0, 0]
                    ot_st = [0, 0]
                    strips = [(h4, qt) for h4 in range(4) for qt in range(32)]
                    p_ready = [0, 0]

                    def emit_SA(i):
                        h4, qt = strips[i]
                        h = 4 * g + h4
                        p0 = (h % 2) * 64
                        lc = h4 // 2
                        su = i % 2
                        Pr.wait("sync", MM, q_mm[su])
                        Pr.dma("sync", QS[su], B3(b_q[su], 128, p0, p0 + 64),
                               sqh[64 * h:64 * h + 64, qt * 128:(qt + 1) * 128])
                        Pr.dma("sync", QS[su], B3(b_q[su] + 128, 128, p0, p0 + 64),
                               sql[64 * h:64 * h + 64, qt * 128:(qt + 1) * 128])
                        L = (qt + 1) * 128
                        KB = (qt + 4) // 4
                        Pr.wait("tensor", QS[su])
                        for kb in range(KB):
                            ks = min(512, L - kb * 512)
                            bsc = kb % 2
                            bank_wait("tensor", bsc)
                            qh_v = B3(b_q[su], 128, p0, p0 + 64)
                            ql_v = B3(b_q[su] + 128, 128, p0, p0 + 64)
                            kh_v = B3(b_kh + lc * S + kb * 512, ks, p0, p0 + 64)
                            kl_v = B3(b_kl + lc * S + kb * 512, ks, p0, p0 + 64)
                            for i3, (wv, rv) in enumerate(((qh_v, kh_v), (qh_v, kl_v), (ql_v, kh_v))):
                                Pr.emit("tensor", lambda e, w=wv, r=rv, bsc=bsc, ks=ks,
                                        f=(i3 == 0), l=(i3 == 2):
                                        e.matmul(ps[:, bsc, 0:ks], w, r, start=f, stop=l),
                                        MM if i3 == 2 else None)
                            q_mm[su] = MM.n
                            if kb == KB - 1:
                                Pr.wait("vector", MM)
                                Pr.wait("vector", E2, strip_e2[su])
                                Pr.wait("vector", EV, strip_ev[su])
                                Pr.wait("vector", MM, strip_mm[su])
                                Pr.emit("vector", lambda e, a=V3(o_p32[su] + kb * 512, ks),
                                        b=ps[:, bsc, 0:ks], mk=V(o_mask + (qt % 4) * 512, ks):
                                        e.tensor_add(a, b, mk), EV)
                            else:
                                Pr.wait("scalar", MM)
                                Pr.wait("scalar", E2, strip_e2[su])
                                Pr.wait("scalar", EV, strip_ev[su])
                                Pr.wait("scalar", MM, strip_mm[su])
                                Pr.emit("scalar", lambda e, a=V3(o_p32[su] + kb * 512, ks),
                                        b=ps[:, bsc, 0:ks]: e.activation(a, b, AF.Copy), EV)
                            bank_free[bsc] = (EV, EV.n)
                            Pr.wait("vector", EV)
                            Pr.emit("vector", lambda e, a=V3(o_bm[su] + kb, 1),
                                    sv_=V3(o_p32[su] + kb * 512, ks):
                                    e.reduce_max(a, sv_, axis=AX), X1)

                    def emit_tail(i):
                        h4, qt = strips[i]
                        h = 4 * g + h4
                        su = i % 2
                        L = (qt + 1) * 128
                        KB = (qt + 4) // 4
                        Pr.wait("vector", X1)
                        Pr.emit("vector", lambda e, a=V3(o_nm, 1), b=V3(o_bm[su], KB):
                                e.reduce_max(a, b, axis=AX, negate=True), X1)
                        Pr.wait("scalar", X1)
                        Pr.emit("scalar", lambda e, a=V3(o_p32[su], L), sv_=V3(o_p32[su], L),
                                nm=V3(o_nm, 1), z=V3(o_z, 1):
                                e.activation(a, sv_, AF.Exp, bias=nm, scale=1.0,
                                             accum_out=z), E2)
                        Pr.wait("vector", E2)
                        Pr.emit("vector", lambda e, a=V3(o_rc[su], 1), b=V3(o_z, 1):
                                e.reciprocal(a, b), X1)
                        Pr.wait("scalar", X1)
                        Pr.emit("scalar", lambda e, a=B3(b_ph[su], L), sv_=V3(o_p32[su], L),
                                s=V3(o_rc[su], 1):
                                e.activation(a, sv_, AF.Copy, scale=s), E2)
                        strip_e2[su] = E2.n
                        Pr.wait("vector", E2)
                        Pr.emit("vector", lambda e, a=B3(b_pl[su], L), p32=V3(o_p32[su], L),
                                s=V3(o_rc[su], 1), hh=B3(b_ph[su], L):
                                e.scalar_tensor_tensor(a, p32, s, hh,
                                                       op0=ALU.mult, op1=ALU.subtract), EV)
                        strip_ev[su] = EV.n
                        nt = qt + 1
                        Pr.wait("scalar", EV)
                        Pr.wait("scalar", MM, strip_mm[su])
                        pth_out = t3b[:, b_pt[su]:b_pt[su] + L].rearrange(
                            "p (t c) -> p t c", c=128)
                        ptl_out = t3b[:, b_pt[su] + S:b_pt[su] + S + L].rearrange(
                            "p (t c) -> p t c", c=128)
                        Pr.dma_t("scalar", GS[su], pth_out, B3(b_ph[su], L))
                        Pr.dma_t("scalar", GS[su], ptl_out, B3(b_pl[su], L))
                        Pr.wait("tensor", GS[su])
                        bav = 6 + (i % 2)
                        bank_wait("tensor", bav)
                        for jt in range(qt + 1):
                            vh_v = B3(b_vh + jt * 256 + 64 * h4, 64)
                            vl_v = B3(b_vl + jt * 256 + 64 * h4, 64)
                            pth = B3(b_pt[su] + jt * 128, 128)
                            ptl = B3(b_pt[su] + S + jt * 128, 128)
                            for i3, (wv, rv) in enumerate(((vh_v, pth), (vl_v, pth), (vh_v, ptl))):
                                Pr.emit("tensor", lambda e, w=wv, r=rv, bav=bav,
                                        f=(jt == 0 and i3 == 0), l=(jt == qt and i3 == 2):
                                        e.matmul(ps[0:64, bav, 0:128], w, r, start=f, stop=l),
                                        MM if (jt == qt and i3 == 2) else None)
                        strip_mm[su] = MM.n
                        ro = i % 2
                        Pr.wait("vector", MM)
                        Pr.wait("vector", SS[ro], ot_st[ro])
                        Pr.emit("vector", lambda e, a=V3(o_ot[ro], 128, 0, 64), bav=bav:
                                e.tensor_copy(a, ps[0:64, bav, 0:128]), EV)
                        bank_free[bav] = (EV, EV.n)
                        Pr.wait("gpsimd", EV)
                        ot_st[ro] = Pr.dma("gpsimd", SS[ro],
                                           so[64 * h:64 * h + 64, qt * 128:(qt + 1) * 128],
                                           V3(o_ot[ro], 128, 0, 64))
                        if upto == 3:
                            ot_st[ro] = Pr.dma("gpsimd", SS[ro],
                                               out[64 * h:64 * h + 64, qt * 128:(qt + 1) * 128],
                                               V3(o_ot[ro], 128, 0, 64))
                        if debug is True:
                            ot_st[ro] = Pr.dma("gpsimd", SS[ro],
                                               dbg["do"][64 * h:64 * h + 64, qt * 128:(qt + 1) * 128],
                                               V3(o_ot[ro], 128, 0, 64))

                    for i in range(len(strips) + 1):
                        if i < len(strips):
                            emit_SA(i)
                        if i >= 1:
                            emit_tail(i - 1)
                    drain_stores()
                    Pr.flush(nc)

        def emit_p4():
            with nc.sbuf_tensor("p4f", [128, 43520], F32) as t4:
                a = Arena(t4, 43520)
                o_o = a.alloc(DC * P)
                o_x = a.alloc(DC * P)
                o_hs = a.alloc(DC * P)
                o_hs2 = a.alloc(DC * P)
                o_h2 = a.alloc(DC * P)
                o_mid = a.alloc(HC * P)
                o_wr = [a.alloc(1024) for _ in range(6)]

                def V4(o, n, p0=0, p1=128):
                    return t4[p0:p1, o:o + n]
                FO = lambda c: V4(o_o + c * P, P)
                FXx = lambda c: V4(o_x + c * P, P)
                FHS = lambda c: V4(o_hs + c * P, P)
                FHS2 = lambda c: V4(o_hs2 + c * P, P)
                FH2 = lambda c: V4(o_h2 + c * P, P)
                FMID = lambda c: V4(o_mid + c * P, P)
                WR4 = lambda slot, sl: V4(o_wr[slot], 1024)[:, sl]

                piece_free = []
                ev_prev = [0]
                for p in range(NP):
                    tsl = slice(p * P, (p + 1) * P)
                    lds = []
                    Pr.wait("sync", EV, ev_prev[0])
                    for c in range(DC):
                        lds.append(Pr.dma("sync", LD, FO(c), so[c * 128:(c + 1) * 128, tsl]))
                        lds.append(Pr.dma("sync", LD, FXx(c), sx1[c * 128:(c + 1) * 128, tsl]))
                        lds.append(Pr.dma("sync", LD, FHS(c), shs[c * 128:(c + 1) * 128, tsl]))
                    Pr.wait("tensor", LD, lds[-1])
                    Pr.wait("vector", LD, lds[-1])
                    Pr.wait("scalar", LD, lds[-1])
                    for i in range(4):
                        Pr.wait("vector", SS[i])
                        Pr.wait("scalar", SS[i])
                    Pr.wait("vector", MM)
                    Pr.wait("scalar", MM)

                    def ep_cp(o, b):
                        Pr.wait("vector", MM)
                        Pr.emit("vector", lambda e, a=FHS2(o), b=b, s=bcol("cp", o), hsv=FHS(o):
                                e.scalar_tensor_tensor(a, ps[:, b, :], s, hsv,
                                                       op0=ALU.add, op1=ALU.add), EV)
                        bank_free[b] = (EV, EV.n)

                    emit_lin(wgen["cp"], DC, DC, FO, WR4, ep_cp, piece_free, ring_n=6)
                    Pr.wait("scalar", EV)
                    emit_ln(FHS2, FH2, FH2, gname="ln2")
                    Pr.wait("tensor", EV)

                    def ep_cf(o, b):
                        Pr.wait("scalar", MM)
                        Pr.emit("scalar", lambda e, a=FMID(o), b=b, s=bcol("cf", o):
                                e.activation(a, ps[:, b, :], AF.Gelu_apprx_tanh,
                                             bias=s, scale=1.0), E2)
                        bank_free[b] = (E2, E2.n)

                    emit_lin(wgen["cf"], HC, DC, FH2, WR4, ep_cf, piece_free, ring_n=6)
                    Pr.wait("tensor", E2)

                    def ep_cm(o, b):
                        sslot = o % 4
                        Pr.wait("vector", MM)
                        Pr.wait("vector", SS[sslot])
                        Pr.emit("vector", lambda e, a=FH2(o), b=b, s=bcol("cm", o), hs2v=FHS2(o):
                                e.scalar_tensor_tensor(a, ps[:, b, :], s, hs2v,
                                                       op0=ALU.add, op1=ALU.add))
                        Pr.emit("vector", lambda e, a=FH2(o), xv=FXx(o):
                                e.tensor_add(a, a, xv), EV)
                        bank_free[b] = (EV, EV.n)
                        Pr.wait("gpsimd", EV)
                        tgt = sx2 if upto >= 5 else out
                        Pr.dma("gpsimd", SS[sslot], tgt[o * 128:(o + 1) * 128, tsl], FH2(o))
                        if debug is True:
                            Pr.dma("gpsimd", SS[sslot], dbg["dx2"][o * 128:(o + 1) * 128, tsl], FH2(o))

                    emit_lin(wgen["cm"], DC, HC, FMID, WR4, ep_cm, piece_free, ring_n=6)
                    ev_prev = [EV.n]
                drain_stores()
                Pr.flush(nc)

        if upto >= 1:
            emit_ffn(x0, sx1 if upto >= 2 else out)
        if upto >= 2:
            emit_p2()
        if upto >= 3:
            emit_p3()
        if upto >= 4:
            emit_p4()
        if upto >= 5:
            emit_ffn(sx2, out)

        drain_stores()
        Pr.flush(nc)

    return nc, dbg


def _slab(w):
    O, Din = w.shape
    oc, dcm = O // 128, Din // 128
    return np.ascontiguousarray(
        w.reshape(oc, 128, dcm, 128).transpose(0, 3, 2, 1).reshape(oc, 128, Din))


def _cols(v):
    O = v.shape[0]
    return np.ascontiguousarray(v.reshape(O // 128, 128).T)


def make_inputs(inp, core):
    g = lambda k: np.asarray(inp[k])
    m = {}
    m["x"] = np.ascontiguousarray(g("L0_attn_hs")[core].T)
    m["w1s"] = _slab(g("L0_ffn_w1_w")); m["w2s"] = _slab(g("L0_ffn_w2_w"))
    m["h1s"] = _slab(g("L0_ffn_h1_w")); m["h2s"] = _slab(g("L0_ffn_h2_w"))
    m["w3s"] = _slab(g("L0_ffn_w3_w"))
    m["s1c"] = _cols(g("L0_ffn_w1_s")); m["s2c"] = _cols(g("L0_ffn_w2_s"))
    m["s1hc"] = _cols(g("L0_ffn_h1_s")); m["s2hc"] = _cols(g("L0_ffn_h2_s"))
    m["s3c"] = _cols(g("L0_ffn_w3_s"))
    for nm in ("ca", "cp", "cf", "cm"):
        sub = g(f"L0_attn_{nm}_sub_w")
        m[f"{nm}_sub0"] = _slab(sub[0]); m[f"{nm}_sub1"] = _slab(sub[1])
        m[f"{nm}_Ww"] = _slab(g(f"L0_attn_{nm}_W_w"))
        m[f"{nm}_Bw"] = _slab(g(f"L0_attn_{nm}_B_w"))
        m[f"{nm}_wlw"] = np.ascontiguousarray(
            np.broadcast_to(g(f"L0_attn_{nm}_Wl_w")[0], (128, 2)))
        bsub = g(f"L0_attn_{nm}_sub_b")
        m[f"{nm}_bsub0"] = _cols(bsub[0]); m[f"{nm}_bsub1"] = _cols(bsub[1])
        m[f"{nm}_bWw"] = _cols(g(f"L0_attn_{nm}_W_b"))
        m[f"{nm}_bBw"] = _cols(g(f"L0_attn_{nm}_B_b"))
        m[f"{nm}_wlb"] = np.ascontiguousarray(
            np.broadcast_to(g(f"L0_attn_{nm}_Wl_b")[0], (128, 2)))
    m["ln1g"] = _cols(g("L0_attn_ln1_g")); m["ln1b"] = _cols(g("L0_attn_ln1_b"))
    m["ln2g"] = _cols(g("L0_attn_ln2_g")); m["ln2b"] = _cols(g("L0_attn_ln2_b"))
    m["identc"] = np.eye(128, dtype=np.float32)
    m["ones128"] = np.ones((128, 128), np.float32)
    mk = np.zeros((128, 2048), np.float32)
    for r in range(4):
        qpos = np.arange(128)[:, None] + 128 * r
        kpos = np.arange(512)[None, :]
        mk[:, r * 512:(r + 1) * 512] = np.where(kpos > qpos, -10000.0, 0.0)
    m["maskc"] = mk
    m["epsrow"] = np.full((1, 1), EPS, np.float32)
    return m


_CACHE = {}
BUILD_UPTO = 5


def kernel(**inputs):
    if "nc" not in _CACHE:
        _CACHE["nc"], _ = build(upto=BUILD_UPTO)
    nc = _CACHE["nc"]
    in_maps = [make_inputs(inputs, c) for c in range(NCORES)]
    res = run_bass_kernel_spmd(nc, in_maps, core_ids=list(range(NCORES)))
    outs = [np.ascontiguousarray(np.asarray(r["out"]).T) for r in res.results]
    return np.stack(outs, axis=0)


# version 06
# speedup vs baseline: 1.2045x; 1.2045x over previous
import contextlib
import numpy as np
import concourse.bass as bass
import concourse.mybir as mybir
from concourse.bass_utils import run_bass_kernel_spmd

F32 = mybir.dt.float32
BF16 = mybir.dt.bfloat16
F8 = mybir.dt.float8e4
AF = mybir.ActivationFunctionType
ALU = mybir.AluOpType
AX = mybir.AxisListType.X

D = 1024
S = 4096
HID = 4096
H = 16
HD = 64
P = 512
NP = S // P
DC = D // 128
HC = HID // 128
EPS = 1e-5
NCORES = 8
WRN = 8

ENGINES = ("sync", "scalar", "vector", "tensor", "gpsimd")


class Sem:
    __slots__ = ("h", "n")
    def __init__(self, handle):
        self.h = handle
        self.n = 0


class Prog:
    def __init__(self):
        self.q = {e: [] for e in ENGINES}

    def emit(self, eng, fn, sem=None, amt=1):
        if sem is not None:
            sem.n += amt
            self.q[eng].append(lambda e, fn=fn, h=sem.h, a=amt: fn(e).then_inc(h, a))
        else:
            self.q[eng].append(fn)
        return sem.n if sem is not None else None

    def dma(self, eng, sem, out, in_):
        sem.n += 16
        self.q[eng].append(
            lambda e, o=out, i=in_, h=sem.h: e.dma_start(out=o, in_=i).then_inc(h, 16))
        return sem.n

    def dma_t(self, eng, sem, out, in_):
        sem.n += 16
        self.q[eng].append(
            lambda e, o=out, i=in_, h=sem.h: e.dma_start(out=o, in_=i, transpose=True).then_inc(h, 16))
        return sem.n

    def wait(self, eng, sem, n=None):
        tgt = sem.n if n is None else n
        if tgt > 0:
            self.q[eng].append(lambda e, h=sem.h, t=tgt: e.wait_ge(h, t))

    def flush(self, nc):
        with nc.Block() as block:
            for name in ENGINES:
                ops = self.q[name]
                if ops:
                    getattr(block, name)(lambda e, ops=ops: [f(e) for f in ops])
        self.q = {e: [] for e in ENGINES}


class Arena:
    def __init__(self, t, total):
        self.t = t
        self.off = 0
        self.total = total

    def alloc(self, n, align=16):
        o = (self.off + align - 1) // align * align
        self.off = o + n
        assert self.off <= self.total, f"arena overflow {self.off} > {self.total}"
        return o


def build(upto=5, debug=False):
    nc = bass.Bass("TRN2", target_bir_lowering=False)

    x0 = nc.dram_tensor("x", [D, S], F32, kind="ExternalInput").ap()
    FFN_MATS = (("w1", HC, DC), ("w2", HC, DC), ("h1", HC, HC), ("h2", HC, HC), ("w3", DC, HC))
    wslab = {nm: nc.dram_tensor(nm + "s", [oc, 128, dcm * 128], F32, kind="ExternalInput").ap()
             for nm, oc, dcm in FFN_MATS}
    scols = {nm: nc.dram_tensor(nm, [128, w], F32, kind="ExternalInput").ap()
             for nm, w in (("s1c", HC), ("s2c", HC), ("s1hc", HC), ("s2hc", HC), ("s3c", DC))}
    gen_in = {}
    GEN_MATS = (("ca", 24, DC), ("cp", DC, DC), ("cf", HC, DC), ("cm", DC, HC))
    for nm, oc, dcm in GEN_MATS:
        for part in ("sub0", "sub1", "Ww", "Bw"):
            gen_in[f"{nm}_{part}"] = nc.dram_tensor(
                f"{nm}_{part}", [oc, 128, dcm * 128], F32, kind="ExternalInput").ap()
        gen_in[f"{nm}_wlw"] = nc.dram_tensor(f"{nm}_wlw", [128, 2], F32, kind="ExternalInput").ap()
        for part in ("bsub0", "bsub1", "bWw", "bBw"):
            gen_in[f"{nm}_{part}"] = nc.dram_tensor(
                f"{nm}_{part}", [128, oc], F32, kind="ExternalInput").ap()
        gen_in[f"{nm}_wlb"] = nc.dram_tensor(f"{nm}_wlb", [128, 2], F32, kind="ExternalInput").ap()
    lncols = {nm: nc.dram_tensor(nm, [128, DC], F32, kind="ExternalInput").ap()
              for nm in ("ln1g", "ln1b", "ln2g", "ln2b")}
    identc = nc.dram_tensor("identc", [128, 128], F32, kind="ExternalInput").ap()
    ones128c = nc.dram_tensor("ones128", [128, 128], F32, kind="ExternalInput").ap()
    maskc = nc.dram_tensor("maskc", [128, 2048], F32, kind="ExternalInput").ap()
    epsrowc = nc.dram_tensor("epsrow", [1, 1], F32, kind="ExternalInput").ap()

    out = nc.dram_tensor("out", [D, S], F32, kind="ExternalOutput").ap()
    sx1 = nc.dram_tensor("sx1", [D, S], F32).ap()
    shs = nc.dram_tensor("shs", [D, S], F32).ap()
    sx2 = nc.dram_tensor("sx2", [D, S], F32).ap()
    sqh = nc.dram_tensor("sqh", [D, S], BF16).ap()
    sql = nc.dram_tensor("sql", [D, S], BF16).ap()
    skh = nc.dram_tensor("skh", [D, S], BF16).ap()
    skl = nc.dram_tensor("skl", [D, S], BF16).ap()
    svh = nc.dram_tensor("svh", [S, D], BF16).ap()
    svl = nc.dram_tensor("svl", [S, D], BF16).ap()
    so = nc.dram_tensor("so", [D, S], F32).ap()
    wgen = {"ca": nc.dram_tensor("wca", [24, 128, D], F32).ap(),
            "cp": nc.dram_tensor("wcp", [DC, 128, D], F32).ap(),
            "cf": nc.dram_tensor("wcf", [HC, 128, D], F32).ap(),
            "cm": nc.dram_tensor("wcm", [DC, 128, HID], F32).ap()}
    wf8 = {nm: nc.dram_tensor(nm + "f8", [oc, 128, dcm * 128], F8).ap()
           for nm, oc, dcm in FFN_MATS}

    dbg = {}
    if debug is True:
        for nm, shp in (("dhs", [D, S]), ("do", [D, S]), ("dx2", [D, S])):
            dbg[nm] = nc.dram_tensor(nm, shp, F32, kind="ExternalOutput").ap()

    with contextlib.ExitStack() as _stk:
        GN = 8192
        garena = _stk.enter_context(nc.sbuf_tensor("garena", [128, GN], F32))
        ps = _stk.enter_context(nc.psum_tensor("ps", [128, 8, 512], F32))

        def _sem(name):
            return Sem(_stk.enter_context(nc.semaphore(name)))
        LD, MM, EV, E2, ST, X1, X2, KV = (
            _sem(n) for n in ("LD", "MM", "EV", "E2", "ST", "X1", "X2", "KV"))
        WS = [_sem(f"WS{i}") for i in range(WRN)]
        QS = [_sem(f"QS{i}") for i in range(2)]
        GS = [_sem(f"GS{i}") for i in range(2)]
        GW = [_sem(f"GW{i}") for i in range(2)]
        SS = [_sem(f"SS{i}") for i in range(4)]
        A = Arena(garena, GN)
        Pr = Prog()

        def V(o, n, p0=0, p1=128):
            return garena[p0:p1, o:o + n]

        o_ident = A.alloc(128)
        o_ones128 = A.alloc(128)
        o_mask = A.alloc(2048)
        o_eps = A.alloc(16)
        o_z4 = A.alloc(512)
        o_z5 = A.alloc(512)
        o_sc = {nm: A.alloc(w) for nm, w in
                (("s1c", HC), ("s2c", HC), ("s1hc", HC), ("s2hc", HC), ("s3c", DC), ("s3hc", DC))}
        o_ln = {nm: A.alloc(DC) for nm in ("ln1g", "ln1b", "ln2g", "ln2b")}
        o_bias = {"ca": A.alloc(24), "cp": A.alloc(DC), "cf": A.alloc(HC), "cm": A.alloc(DC)}
        o_bk8 = A.alloc(8)
        o_rows = A.alloc(6 * 512)

        ident = V(o_ident, 128)
        ones128v = V(o_ones128, 128)
        epsv = V(o_eps, 1, 0, 1)

        def row(i, n=512):
            return V(o_rows + i * 512, n, 0, 1)

        def scol(name, i):
            return V(o_sc[name] + i, 1)

        def lncol(name, i):
            return V(o_ln[name] + i, 1)

        def bcol(name, i):
            return V(o_bias[name] + i, 1)

        bank_free = [(EV, 0)] * 8
        ln_state = {"bc_mm": 0}

        def bank_wait(eng, b):
            sem, n = bank_free[b]
            Pr.wait(eng, sem, n)

        def drain_stores():
            Pr.wait("gpsimd", ST)
            for s_ in SS:
                Pr.wait("gpsimd", s_)

        Pr.dma("sync", LD, ident, identc)
        Pr.dma("sync", LD, V(o_mask, 2048), maskc)
        Pr.dma("sync", LD, ones128v, ones128c)
        Pr.dma("sync", LD, epsv, epsrowc)
        for nm in ("s1c", "s2c", "s1hc", "s2hc", "s3c"):
            Pr.dma("sync", LD, V(o_sc[nm], scols[nm].shape[-1]), scols[nm])
        for nm in ("ln1g", "ln1b", "ln2g", "ln2b"):
            Pr.dma("sync", LD, V(o_ln[nm], DC), lncols[nm])
        Pr.wait("vector", LD)
        Pr.emit("vector", lambda e, o1=V(o_sc["s3hc"], DC), o2=V(o_sc["s3c"], DC):
                e.tensor_scalar_mul(o1, o2, 0.5), EV)
        Pr.emit("vector", lambda e, a=V(o_z4, 512): e.memset(a, 0.0))
        Pr.emit("vector", lambda e, a=V(o_z5, 512): e.memset(a, 0.0))
        Pr.flush(nc)

        with (
            nc.sbuf_tensor("p0f", [128, 14592], F32) as t0f,
            nc.sbuf_tensor("p0e", [128, 2048], F8) as t0e,
        ):
            A0 = Arena(t0f, 14592)
            o_wl = A0.alloc(16)
            o_gs0 = [A0.alloc(1024) for _ in range(2)]
            o_gs1 = [A0.alloc(1024) for _ in range(2)]
            o_gww = [A0.alloc(1024) for _ in range(2)]
            o_gbw = [A0.alloc(1024) for _ in range(2)]
            o_gt = [A0.alloc(1024) for _ in range(2)]
            o_gw32 = [A0.alloc(1024) for _ in range(2)]
            o_bt = [A0.alloc(64) for _ in range(4)]

            def V0(o, n, p0=0, p1=128):
                return t0f[p0:p1, o:o + n]

            for nm, oc, dcm in GEN_MATS:
                Pr.wait("sync", EV)
                Pr.dma("sync", LD, V0(o_wl, 2), gen_in[f"{nm}_wlw"])
                npc = dcm * 128 // 1024
                jglob = 0
                for o in range(oc):
                    for pc in range(npc):
                        r = jglob % 2
                        sl = slice(pc * 1024, (pc + 1) * 1024)
                        if jglob >= 2:
                            Pr.wait("sync", SS[r])
                        Pr.dma("sync", GS[r], V0(o_gs0[r], 1024), gen_in[f"{nm}_sub0"][o, :, sl])
                        Pr.dma("sync", GS[r], V0(o_gs1[r], 1024), gen_in[f"{nm}_sub1"][o, :, sl])
                        Pr.dma("sync", GW[r], V0(o_gww[r], 1024), gen_in[f"{nm}_Ww"][o, :, sl])
                        Pr.dma("sync", GW[r], V0(o_gbw[r], 1024), gen_in[f"{nm}_Bw"][o, :, sl])
                        Pr.wait("vector", GS[r])
                        Pr.emit("vector", lambda e, a=V0(o_gt[r], 1024), b=V0(o_gs0[r], 1024),
                                w=V0(o_wl, 1): e.tensor_scalar_mul(a, b, w))
                        Pr.emit("vector", lambda e, a=V0(o_gs0[r], 1024), b=V0(o_gs1[r], 1024),
                                w=V0(o_wl + 1, 1), c=V0(o_gt[r], 1024):
                                e.scalar_tensor_tensor(a, b, w, c, op0=ALU.mult, op1=ALU.add), X1)
                        Pr.wait("scalar", X1)
                        Pr.emit("scalar", lambda e, a=V0(o_gt[r], 1024), b=V0(o_gs0[r], 1024):
                                e.activation(a, b, AF.Tanh), X2)
                        Pr.wait("vector", X2)
                        Pr.wait("vector", GW[r])
                        Pr.emit("vector", lambda e, a=V0(o_gs1[r], 1024), b=V0(o_gt[r], 1024),
                                c=V0(o_gww[r], 1024): e.tensor_mul(a, b, c))
                        Pr.emit("vector", lambda e, a=V0(o_gw32[r], 1024), b=V0(o_gs1[r], 1024),
                                c=V0(o_gbw[r], 1024): e.tensor_add(a, b, c), EV)
                        Pr.wait("gpsimd", EV)
                        Pr.dma("gpsimd", SS[r], wgen[nm][o, :, sl], V0(o_gw32[r], 1024))
                        jglob += 1

            Pr.wait("sync", EV)
            cast_jobs = [(nm, o, pc) for nm, oc, dcm in FFN_MATS
                         for o in range(oc) for pc in range(dcm // 8)]
            for jg, (nm, o, pc) in enumerate(cast_jobs):
                r = jg % 2
                sl = slice(pc * 1024, (pc + 1) * 1024)
                if jg >= 2:
                    Pr.wait("sync", SS[2 + r])
                Pr.dma("sync", GS[r], V0(o_gs0[r], 1024), wslab[nm][o, :, sl])
                Pr.wait("vector", GS[r])
                Pr.emit("vector", lambda e, a=t0e[:, r * 1024:(r + 1) * 1024],
                        b=V0(o_gs0[r], 1024): e.tensor_copy(a, b), EV)
                Pr.wait("gpsimd", EV)
                Pr.dma("gpsimd", SS[2 + r], wf8[nm][o, :, sl], t0e[:, r * 1024:(r + 1) * 1024])

            for nm, oc, dcm in GEN_MATS:
                Pr.wait("sync", EV)
                Pr.dma("sync", LD, V0(o_wl, 2), gen_in[f"{nm}_wlb"])
                Pr.dma("sync", LD, V0(o_bt[0], oc), gen_in[f"{nm}_bsub0"])
                Pr.dma("sync", LD, V0(o_bt[1], oc), gen_in[f"{nm}_bsub1"])
                Pr.dma("sync", LD, V0(o_bt[2], oc), gen_in[f"{nm}_bWw"])
                Pr.dma("sync", LD, V0(o_bt[3], oc), gen_in[f"{nm}_bBw"])
                Pr.wait("vector", LD)
                ob = V(o_bias[nm], oc)
                Pr.emit("vector", lambda e, a=ob, b=V0(o_bt[0], oc), w=V0(o_wl, 1):
                        e.tensor_scalar_mul(a, b, w))
                Pr.emit("vector", lambda e, a=ob, b=V0(o_bt[1], oc), w=V0(o_wl + 1, 1):
                        e.scalar_tensor_tensor(a, b, w, a, op0=ALU.mult, op1=ALU.add), X1)
                Pr.wait("scalar", X1)
                Pr.emit("scalar", lambda e, a=ob: e.activation(a, a, AF.Tanh), X2)
                Pr.wait("vector", X2)
                Pr.emit("vector", lambda e, a=ob, b=V0(o_bt[2], oc): e.tensor_mul(a, a, b))
                Pr.emit("vector", lambda e, a=ob, b=V0(o_bt[3], oc): e.tensor_add(a, a, b), EV)
            Pr.emit("vector", lambda e, a=V(o_bk8, 8), b=V(o_bias["ca"] + 8, 8):
                    e.tensor_scalar_mul(a, b, 0.125), EV)
            drain_stores()
            Pr.flush(nc)

        def emit_ln(srcf, dstf, sqf, gname=None, width=P):
            Pr.wait("scalar", EV)
            for c in range(DC):
                Pr.emit("scalar", lambda e, a=sqf(c), b=srcf(c): e.activation(a, b, AF.Square), E2)
            bank_wait("tensor", 6)
            bank_wait("tensor", 7)
            for c in range(DC):
                Pr.wait("tensor", E2)
                Pr.emit("tensor", lambda e, r=srcf(c), c=c:
                        e.matmul(ps[:, 6, 0:width], ones128v, r,
                                 start=(c == 0), stop=(c == DC - 1)),
                        MM if c == DC - 1 else None)
                Pr.emit("tensor", lambda e, r=sqf(c), c=c:
                        e.matmul(ps[:, 7, 0:width], ones128v, r,
                                 start=(c == 0), stop=(c == DC - 1)),
                        MM if c == DC - 1 else None)
            Pr.wait("vector", MM)
            Pr.emit("vector", lambda e, a=row(0, width): e.tensor_scalar_mul(a, ps[0:1, 6, 0:width], 1.0 / D))
            Pr.emit("vector", lambda e, a=row(1, width): e.tensor_scalar_mul(a, ps[0:1, 7, 0:width], 1.0 / D), X1)
            Pr.wait("vector", X1)
            Pr.emit("vector", lambda e, a=row(2, width), m=row(0, width): e.tensor_mul(a, m, m), X1)
            Pr.wait("vector", X1)
            Pr.emit("vector", lambda e, a=row(2, width), b=row(1, width), c=row(2, width):
                    e.tensor_sub(a, b, c), X1)
            bank_free[6] = (X1, X1.n)
            bank_free[7] = (X1, X1.n)
            Pr.wait("scalar", X1)
            Pr.emit("scalar", lambda e, a=row(3, width), b=row(2, width):
                    e.activation(a, b, AF.Sqrt, bias=epsv, scale=1.0), X2)
            Pr.wait("vector", X2)
            Pr.emit("vector", lambda e, a=row(4, width), b=row(3, width): e.reciprocal(a, b), X1)
            Pr.wait("vector", X1)
            Pr.emit("vector", lambda e, a=row(5, width), m=row(0, width), r=row(4, width):
                    e.tensor_mul(a, m, r), X1)
            Pr.wait("vector", X1)
            Pr.emit("vector", lambda e, a=row(5, width): e.tensor_scalar_mul(a, a, -1.0), X1)
            Pr.wait("vector", X1)
            Pr.wait("vector", MM, ln_state["bc_mm"])
            Pr.emit("vector", lambda e, a=V(o_z4, width, 0, 1), b=row(4, width): e.tensor_copy(a, b))
            Pr.emit("vector", lambda e, a=V(o_z5, width, 0, 1), b=row(5, width): e.tensor_copy(a, b), X1)
            Pr.wait("tensor", X1)
            bank_wait("tensor", 4)
            bank_wait("tensor", 5)
            Pr.emit("tensor", lambda e, r=V(o_z4, width): e.matmul(
                ps[:, 4, 0:width], ones128v, r, start=True, stop=True))
            Pr.emit("tensor", lambda e, r=V(o_z5, width): e.matmul(
                ps[:, 5, 0:width], ones128v, r, start=True, stop=True), MM)
            ln_state["bc_mm"] = MM.n
            Pr.wait("vector", MM)
            for c in range(DC):
                last = (c == DC - 1)
                Pr.emit("vector", lambda e, a=dstf(c), b=srcf(c): e.tensor_mul(a, b, ps[:, 4, 0:width]))
                if gname is None:
                    Pr.emit("vector", lambda e, a=dstf(c):
                            e.tensor_add(a, a, ps[:, 5, 0:width]), EV if last else None)
                else:
                    Pr.emit("vector", lambda e, a=dstf(c):
                            e.tensor_add(a, a, ps[:, 5, 0:width]))
                    Pr.emit("vector", lambda e, a=dstf(c),
                            g=lncol(gname + "g", c), b=lncol(gname + "b", c):
                            e.tensor_scalar(a, a, g, b, op0=ALU.mult, op1=ALU.add),
                            EV if last else None)
            bank_free[4] = (EV, EV.n)
            bank_free[5] = (EV, EV.n)

        def emit_lin(wdram, oc_n, dcm, rhsf, wringf, epilogue, piece_free,
                     banks=(0, 1, 2, 3), rhsf2=None, ring_n=WRN, ws=None):
            if ws is None:
                ws = WS
            ppo = dcm // 8
            nb = len(banks)
            for o in range(oc_n):
                j0 = len(piece_free)
                for q in range(ppo):
                    j = j0 + q
                    if j >= ring_n:
                        Pr.wait("sync", MM, piece_free[j - ring_n])
                    Pr.dma("sync", ws[j % ring_n], wringf(j % ring_n, slice(0, 1024)),
                           wdram[o, :, q * 1024:(q + 1) * 1024])
                b = banks[o % nb]
                for q in range(ppo):
                    Pr.wait("tensor", ws[(j0 + q) % ring_n])
                bank_wait("tensor", b)
                npass = 2 if rhsf2 is not None else 1
                for c in range(dcm):
                    jj = (j0 + c // 8) % ring_n
                    wv = wringf(jj, slice((c % 8) * 128, (c % 8) * 128 + 128))
                    for pss in range(npass):
                        rv = rhsf(c) if pss == 0 else rhsf2(c)
                        first = (c == 0 and pss == 0)
                        lastm = (c == dcm - 1 and pss == npass - 1)
                        Pr.emit("tensor", lambda e, w=wv, r=rv, b=b, f=first, l=lastm:
                                e.matmul(ps[:, b, :], w, r, start=f, stop=l),
                                MM if lastm else None)
                for _ in range(ppo):
                    piece_free.append(MM.n)
                epilogue(o, b)

        ffn_id = [0]

        def emit_ffn(x_in_dram, x_out_dram):
            fid = ffn_id[0]; ffn_id[0] += 1
            with (
                nc.sbuf_tensor(f"ffa{fid}", [128, 8704], F32) as tfa,
                nc.sbuf_tensor(f"ffb{fid}", [128, 65536], BF16) as tfb,
                nc.sbuf_tensor(f"ff8{fid}", [128, WRN * 1024], F8) as tf8,
            ):
                o_x = 0
                o_sh = DC * P
                FX = lambda c: tfa[:, o_x + c * P: o_x + (c + 1) * P]
                SH = lambda off, n=P: tfa[:, o_sh + off: o_sh + off + n]

                def HAh(c): return tfb[:, c * P:(c + 1) * P]
                def HAl(c): return tfb[:, 16384 + c * P: 16384 + (c + 1) * P]
                def HBh(c): return tfb[:, 32768 + c * P: 32768 + (c + 1) * P]
                def HBl(c): return tfb[:, 49152 + c * P: 49152 + (c + 1) * P]
                XNh = HBh
                def XNl(c): return tfb[:, 32768 + DC * P + c * P: 32768 + (DC + c + 1) * P]
                WR = lambda slot, sl: tf8[:, slot * 1024:(slot + 1) * 1024][:, sl]

                ev_prev_x = [0] * DC
                piece_free = []
                for p in range(NP):
                    tsl = slice(p * P, (p + 1) * P)
                    ldx = []
                    for c in range(DC):
                        Pr.wait("sync", EV, ev_prev_x[c])
                        ldx.append(Pr.dma("sync", LD, FX(c),
                                          x_in_dram[c * 128:(c + 1) * 128, tsl]))
                    Pr.wait("scalar", LD, ldx[-1])
                    Pr.wait("vector", LD, ldx[-1])
                    Pr.wait("tensor", LD, ldx[-1])
                    for i in range(4):
                        Pr.wait("vector", SS[i])
                        Pr.wait("scalar", SS[i])
                    Pr.wait("vector", MM)
                    Pr.wait("scalar", MM)
                    emit_ln(FX, lambda c: SH(c * 512), lambda c: SH(c * 512))
                    for c in range(DC):
                        Pr.emit("scalar", lambda e, a=XNh(c), b=SH(c * 512):
                                e.activation(a, b, AF.Copy), E2)
                        Pr.wait("vector", E2)
                        Pr.emit("vector", lambda e, a=XNl(c), b=SH(c * 512), h=XNh(c):
                                e.tensor_sub(a, b, h), EV)
                    split_ev = EV.n
                    t1_x2 = [0, 0]
                    g32_ev = [0, 0]
                    Pr.wait("tensor", EV, split_ev)
                    for o in range(HC):
                        for mat, wd in ((0, wf8["w1"]), (1, wf8["w2"])):
                            j = len(piece_free)
                            if j >= WRN:
                                Pr.wait("sync", MM, piece_free[j - WRN])
                            Pr.dma("sync", WS[j % WRN], WR(j % WRN, slice(0, 1024)), wd[o, :, :])
                            b = (o % 2) * 2 + mat
                            Pr.wait("tensor", WS[j % WRN])
                            bank_wait("tensor", b)
                            for c in range(DC):
                                for pss in range(2):
                                    rv = XNh(c) if pss == 0 else XNl(c)
                                    Pr.emit("tensor", lambda e,
                                            w=WR(j % WRN, slice(c * 128, c * 128 + 128)),
                                            r=rv, b=b, f=(c == 0 and pss == 0),
                                            l=(c == DC - 1 and pss == 1):
                                            e.matmul(ps[:, b, :], w, r, start=f, stop=l),
                                            MM if (c == DC - 1 and pss == 1) else None)
                            piece_free.append(MM.n)
                            if mat == 0:
                                mm1 = MM.n
                            else:
                                mm2 = MM.n
                        b1 = (o % 2) * 2
                        b2 = b1 + 1
                        rt = o % 2
                        Pr.wait("scalar", MM, mm1)
                        Pr.wait("scalar", X2, t1_x2[rt])
                        Pr.emit("scalar", lambda e, a=SH(rt * 512), b=b1, s=scol("s1c", o):
                                e.activation(a, ps[:, b, :], AF.Silu, scale=s), E2)
                        bank_free[b1] = (E2, E2.n)
                        Pr.wait("vector", E2)
                        Pr.wait("vector", MM, mm2)
                        Pr.wait("vector", EV, g32_ev[rt])
                        Pr.emit("vector", lambda e, a=SH(1024 + rt * 512), b=b2, s=scol("s2c", o),
                                t=SH(rt * 512):
                                e.scalar_tensor_tensor(a, ps[:, b, :], s, t,
                                                       op0=ALU.mult, op1=ALU.mult), X2)
                        t1_x2[rt] = X2.n
                        bank_free[b2] = (X2, X2.n)
                        Pr.wait("scalar", X2)
                        Pr.emit("scalar", lambda e, a=HAh(o), b=SH(1024 + rt * 512):
                                e.activation(a, b, AF.Copy), E2)
                        Pr.wait("vector", E2)
                        Pr.emit("vector", lambda e, a=HAl(o), b=SH(1024 + rt * 512), h=HAh(o):
                                e.tensor_sub(a, b, h), EV)
                        g32_ev[rt] = EV.n
                    Pr.wait("tensor", EV)

                    def ep_h(dsth, dstl, sname):
                        st = {"g": [0, 0]}
                        def ep(o, b):
                            rt = o % 2
                            Pr.wait("scalar", MM)
                            Pr.wait("scalar", EV, st["g"][rt])
                            Pr.emit("scalar", lambda e, a=SH(1024 + rt * 512), b=b, s=scol(sname, o):
                                    e.activation(a, ps[:, b, :], AF.Gelu, scale=s), X2)
                            bank_free[b] = (X2, X2.n)
                            Pr.wait("scalar", X2)
                            Pr.emit("scalar", lambda e, a=dsth(o), b=SH(1024 + rt * 512):
                                    e.activation(a, b, AF.Copy), E2)
                            Pr.wait("vector", E2)
                            Pr.emit("vector", lambda e, a=dstl(o), b=SH(1024 + rt * 512), h=dsth(o):
                                    e.tensor_sub(a, b, h), EV)
                            st["g"][rt] = EV.n
                        return ep

                    emit_lin(wf8["h1"], HC, HC, HAh, WR, ep_h(HBh, HBl, "s1hc"),
                             piece_free, rhsf2=HAl)
                    Pr.wait("tensor", EV)
                    emit_lin(wf8["h2"], HC, HC, HBh, WR, ep_h(HAh, HAl, "s2hc"),
                             piece_free, rhsf2=HBl)
                    Pr.wait("tensor", EV)
                    ev_cur_x = [0] * DC

                    def ep_w3(o, b):
                        sslot = o % 4
                        Pr.wait("vector", MM)
                        Pr.wait("vector", SS[sslot])
                        Pr.emit("vector", lambda e, a=SH(2048 + sslot * 512), b=b,
                                s=scol("s3hc", o), x=FX(o):
                                e.scalar_tensor_tensor(a, ps[:, b, :], s, x,
                                                       op0=ALU.mult, op1=ALU.add), EV)
                        bank_free[b] = (EV, EV.n)
                        ev_cur_x[o] = EV.n
                        Pr.wait("gpsimd", EV)
                        Pr.dma("gpsimd", SS[sslot], x_out_dram[o * 128:(o + 1) * 128, tsl],
                               SH(2048 + sslot * 512))

                    emit_lin(wf8["w3"], DC, HC, HAh, WR, ep_w3, piece_free, rhsf2=HAl)
                    ev_prev_x = ev_cur_x
                drain_stores()
                Pr.flush(nc)

        def emit_p2():
            with (
                nc.sbuf_tensor("p2f", [128, 23040], F32) as t2,
                nc.sbuf_tensor("p2b", [128, 4096], BF16) as t2b,
            ):
                a = Arena(t2, 23040)
                o_x = a.alloc(DC * P)
                o_hs = a.alloc(DC * P)
                o_h = a.alloc(DC * P)
                o_wr = [a.alloc(1024) for _ in range(6)]
                o_vt = a.alloc(P)
                o_qo = [a.alloc(P) for _ in range(2)]
                QH = lambda rt: t2b[:, rt * 1024:rt * 1024 + 512]
                QL = lambda rt: t2b[:, rt * 1024 + 512:rt * 1024 + 1024]
                VH = lambda rv: t2b[:, 2048 + rv * 256:2048 + rv * 256 + 128]
                VL = lambda rv: t2b[:, 2048 + rv * 256 + 128:2048 + rv * 256 + 256]

                def V2(o, n, p0=0, p1=128):
                    return t2[p0:p1, o:o + n]
                FXx = lambda c: V2(o_x + c * P, P)
                FHS = lambda c: V2(o_hs + c * P, P)
                FH = lambda c: V2(o_h + c * P, P)

                ev_prev_x = [0] * DC
                piece_free = []
                qo_st = [0, 0]
                vtt_st = [0, 0]
                vt_mm = [0]
                for p in range(NP):
                    tsl = slice(p * P, (p + 1) * P)
                    ldx = []
                    for c in range(DC):
                        Pr.wait("sync", EV, ev_prev_x[c])
                        ldx.append(Pr.dma("sync", LD, FXx(c),
                                          sx1[c * 128:(c + 1) * 128, tsl]))
                    Pr.wait("scalar", LD, ldx[-1])
                    Pr.wait("vector", LD, ldx[-1])
                    Pr.wait("tensor", LD, ldx[-1])
                    Pr.wait("scalar", MM)
                    Pr.wait("scalar", ST)
                    Pr.wait("vector", MM)
                    emit_ln(FXx, FHS, FHS)
                    for c in range(DC):
                        ev_prev_x[c] = EV.n
                    Pr.wait("gpsimd", EV)
                    for c in range(DC):
                        Pr.dma("gpsimd", ST, shs[c * 128:(c + 1) * 128, tsl], FHS(c))
                        if upto == 2:
                            Pr.dma("gpsimd", ST, out[c * 128:(c + 1) * 128, tsl], FHS(c))
                        if debug is True:
                            Pr.dma("gpsimd", ST, dbg["dhs"][c * 128:(c + 1) * 128, tsl], FHS(c))
                    emit_ln(FHS, FH, FH, gname="ln1")
                    for o in range(24):
                        j = len(piece_free)
                        if j >= 6:
                            Pr.wait("sync", MM, piece_free[j - 6])
                        Pr.dma("sync", WS[j % 6], V2(o_wr[j % 6], 1024), wgen["ca"][o, :, :])
                        b = o % 4
                        Pr.wait("tensor", WS[j % 6])
                        bank_wait("tensor", b)
                        if o == 0:
                            Pr.wait("tensor", EV)
                        for c in range(DC):
                            Pr.emit("tensor", lambda e, w=V2(o_wr[j % 6] + c * 128, 128),
                                    r=FH(c), b=b, c=c:
                                    e.matmul(ps[:, b, :], w, r, start=(c == 0), stop=(c == DC - 1)),
                                    MM if c == DC - 1 else None)
                        piece_free.append(MM.n)
                        rt = o % 2
                        if o < 16:
                            Pr.wait("vector", MM)
                            Pr.wait("vector", SS[rt], qo_st[rt])
                            if o < 8:
                                Pr.emit("vector", lambda e, a=V2(o_qo[rt], P), b=b, s=bcol("ca", o):
                                        e.tensor_scalar(a, ps[:, b, :], s, None, op0=ALU.add), EV)
                                dsth, dstl, drow = sqh, sql, o
                            else:
                                Pr.emit("vector", lambda e, a=V2(o_qo[rt], P), b=b,
                                        s=V(o_bk8 + (o - 8), 1):
                                        e.tensor_scalar(a, ps[:, b, :], 0.125, s,
                                                        op0=ALU.mult, op1=ALU.add), EV)
                                dsth, dstl, drow = skh, skl, o - 8
                            bank_free[b] = (EV, EV.n)
                            Pr.wait("scalar", EV)
                            Pr.emit("scalar", lambda e, a=QH(rt), b=V2(o_qo[rt], P):
                                    e.activation(a, b, AF.Copy), E2)
                            Pr.wait("vector", E2)
                            Pr.emit("vector", lambda e, a=QL(rt), b=V2(o_qo[rt], P), h=QH(rt):
                                    e.tensor_sub(a, b, h), EV)
                            Pr.wait("gpsimd", EV)
                            Pr.dma("gpsimd", SS[rt], dsth[drow * 128:(drow + 1) * 128, tsl], QH(rt))
                            qo_st[rt] = Pr.dma("gpsimd", SS[rt],
                                               dstl[drow * 128:(drow + 1) * 128, tsl], QL(rt))
                        else:
                            Pr.wait("vector", MM)
                            Pr.wait("vector", MM, vt_mm[0])
                            Pr.emit("vector", lambda e, a=V2(o_vt, P), b=b, s=bcol("ca", o):
                                    e.tensor_scalar(a, ps[:, b, :], s, None, op0=ALU.add), EV)
                            bank_free[b] = (EV, EV.n)
                            Pr.wait("tensor", EV)
                            for jt in range(P // 128):
                                bt = 4 + (jt % 2)
                                bank_wait("tensor", bt)
                                Pr.emit("tensor", lambda e, src=V2(o_vt + jt * 128, 128), bt=bt:
                                        e.transpose(ps[:, bt, 0:128], src, ident), MM)
                                vt_mm[0] = MM.n
                                rv = jt % 2
                                Pr.wait("scalar", MM)
                                Pr.wait("scalar", SS[2 + rv], vtt_st[rv])
                                Pr.emit("scalar", lambda e, a=VH(rv), bt=bt:
                                        e.activation(a, ps[:, bt, 0:128], AF.Copy), E2)
                                Pr.wait("vector", E2)
                                Pr.emit("vector", lambda e, a=VL(rv), bt=bt, h=VH(rv):
                                        e.tensor_sub(a, ps[:, bt, 0:128], h), EV)
                                bank_free[bt] = (EV, EV.n)
                                Pr.wait("gpsimd", EV)
                                trow = p * P + jt * 128
                                Pr.dma("gpsimd", SS[2 + rv],
                                       svh[trow:trow + 128, (o - 16) * 128:(o - 15) * 128], VH(rv))
                                vtt_st[rv] = Pr.dma("gpsimd", SS[2 + rv],
                                                    svl[trow:trow + 128, (o - 16) * 128:(o - 15) * 128],
                                                    VL(rv))
                drain_stores()
                Pr.flush(nc)

        def emit_p3():
            with (
                nc.sbuf_tensor("p3f", [128, 8704], F32) as t3,
                nc.sbuf_tensor("p3b", [128, 66560], BF16) as t3b,
            ):
                a = Arena(t3, 8704)
                o_p32 = [a.alloc(S) for _ in range(2)]
                o_nm = a.alloc(16)
                o_bm = [a.alloc(16) for _ in range(2)]
                o_z = a.alloc(16)
                o_rc = [a.alloc(16) for _ in range(2)]
                o_ot = [a.alloc(128) for _ in range(2)]
                ab = Arena(t3b, 66560)
                b_kh = ab.alloc(2 * S)
                b_kl = ab.alloc(2 * S)
                b_vh = ab.alloc(HC * 256)
                b_vl = ab.alloc(HC * 256)
                b_q = [ab.alloc(256) for _ in range(4)]
                b_pt = [ab.alloc(2 * S) for _ in range(2)]
                b_ph = [ab.alloc(S) for _ in range(2)]
                b_pl = [ab.alloc(S) for _ in range(2)]

                def V3(o, n, p0=0, p1=128):
                    return t3[p0:p1, o:o + n]

                def B3(o, n, p0=0, p1=128):
                    return t3b[p0:p1, o:o + n]

                QSL = [QS[0], QS[1], GW[0], GW[1]]
                prev_group_mm = [0]
                for g in range(4):
                    Pr.wait("sync", MM, prev_group_mm[0])
                    for c in range(2):
                        Pr.dma("sync", KV, B3(b_kh + c * S, S),
                               skh[(2 * g + c) * 128:(2 * g + c + 1) * 128, :])
                        Pr.dma("sync", KV, B3(b_kl + c * S, S),
                               skl[(2 * g + c) * 128:(2 * g + c + 1) * 128, :])
                    for kchunk in range(HC):
                        Pr.dma("sync", KV, B3(b_vh + kchunk * 256, 256),
                               svh[kchunk * 128:(kchunk + 1) * 128, 256 * g:256 * (g + 1)])
                        Pr.dma("sync", KV, B3(b_vl + kchunk * 256, 256),
                               svl[kchunk * 128:(kchunk + 1) * 128, 256 * g:256 * (g + 1)])
                    Pr.wait("tensor", KV)
                    strip_mm = [0, 0]
                    strip_e2 = [0, 0]
                    strip_ev = [0, 0]
                    kbc = [0]
                    q_mm = [0, 0, 0, 0]
                    ot_st = [0, 0]
                    strips = [(h4, qt) for h4 in range(4) for qt in range(32)]

                    def emit_QL(i):
                        h4, qt = strips[i]
                        h = 4 * g + h4
                        p0 = (h % 2) * 64
                        rq = i % 4
                        Pr.wait("sync", MM, q_mm[rq])
                        Pr.dma("sync", QSL[rq], B3(b_q[rq], 128, p0, p0 + 64),
                               sqh[64 * h:64 * h + 64, qt * 128:(qt + 1) * 128])
                        Pr.dma("sync", QSL[rq], B3(b_q[rq] + 128, 128, p0, p0 + 64),
                               sql[64 * h:64 * h + 64, qt * 128:(qt + 1) * 128])

                    def emit_SA(i):
                        h4, qt = strips[i]
                        h = 4 * g + h4
                        p0 = (h % 2) * 64
                        lc = h4 // 2
                        su = i % 2
                        rq = i % 4
                        L = (qt + 1) * 128
                        KB = (qt + 4) // 4
                        Pr.wait("tensor", QSL[rq])
                        for kb in range(KB):
                            ks = min(512, L - kb * 512)
                            bsc = kbc[0] % 4
                            kbc[0] += 1
                            bank_wait("tensor", bsc)
                            qh_v = B3(b_q[rq], 128, p0, p0 + 64)
                            ql_v = B3(b_q[rq] + 128, 128, p0, p0 + 64)
                            kh_v = B3(b_kh + lc * S + kb * 512, ks, p0, p0 + 64)
                            kl_v = B3(b_kl + lc * S + kb * 512, ks, p0, p0 + 64)
                            for i3, (wv, rv) in enumerate(((qh_v, kh_v), (qh_v, kl_v), (ql_v, kh_v))):
                                Pr.emit("tensor", lambda e, w=wv, r=rv, bsc=bsc, ks=ks,
                                        f=(i3 == 0), l=(i3 == 2):
                                        e.matmul(ps[:, bsc, 0:ks], w, r, start=f, stop=l),
                                        MM if i3 == 2 else None)
                            q_mm[rq] = MM.n
                            if kb == KB - 1:
                                Pr.wait("vector", MM)
                                Pr.wait("vector", E2, strip_e2[su])
                                Pr.wait("vector", EV, strip_ev[su])
                                Pr.emit("vector", lambda e, a=V3(o_p32[su] + kb * 512, ks),
                                        b=ps[:, bsc, 0:ks], mk=V(o_mask + (qt % 4) * 512, ks):
                                        e.tensor_add(a, b, mk), EV)
                            else:
                                Pr.wait("scalar", MM)
                                Pr.wait("scalar", E2, strip_e2[su])
                                Pr.wait("scalar", EV, strip_ev[su])
                                Pr.emit("scalar", lambda e, a=V3(o_p32[su] + kb * 512, ks),
                                        b=ps[:, bsc, 0:ks]: e.activation(a, b, AF.Copy), EV)
                            bank_free[bsc] = (EV, EV.n)
                            Pr.wait("vector", EV)
                            Pr.emit("vector", lambda e, a=V3(o_bm[su] + kb, 1),
                                    sv_=V3(o_p32[su] + kb * 512, ks):
                                    e.reduce_max(a, sv_, axis=AX), X1)

                    def emit_SBSC(i):
                        h4, qt = strips[i]
                        h = 4 * g + h4
                        su = i % 2
                        L = (qt + 1) * 128
                        KB = (qt + 4) // 4
                        Pr.wait("vector", X1)
                        Pr.emit("vector", lambda e, a=V3(o_nm, 1), b=V3(o_bm[su], KB):
                                e.reduce_max(a, b, axis=AX, negate=True), X1)
                        Pr.wait("scalar", X1)
                        Pr.emit("scalar", lambda e, a=V3(o_p32[su], L), sv_=V3(o_p32[su], L),
                                nm=V3(o_nm, 1), z=V3(o_z, 1):
                                e.activation(a, sv_, AF.Exp, bias=nm, scale=1.0,
                                             accum_out=z), E2)
                        Pr.wait("vector", E2)
                        Pr.emit("vector", lambda e, a=V3(o_rc[su], 1), b=V3(o_z, 1):
                                e.reciprocal(a, b), X1)
                        Pr.wait("scalar", X1)
                        Pr.emit("scalar", lambda e, a=B3(b_ph[su], L), sv_=V3(o_p32[su], L),
                                s=V3(o_rc[su], 1):
                                e.activation(a, sv_, AF.Copy, scale=s), E2)
                        strip_e2[su] = E2.n
                        Pr.wait("vector", E2)
                        Pr.emit("vector", lambda e, a=B3(b_pl[su], L), p32=V3(o_p32[su], L),
                                s=V3(o_rc[su], 1), hh=B3(b_ph[su], L):
                                e.scalar_tensor_tensor(a, p32, s, hh,
                                                       op0=ALU.mult, op1=ALU.subtract), EV)
                        strip_ev[su] = EV.n
                        nt = qt + 1
                        Pr.wait("scalar", EV)
                        Pr.wait("scalar", MM, strip_mm[su])
                        pth_out = t3b[:, b_pt[su]:b_pt[su] + L].rearrange(
                            "p (t c) -> p t c", c=128)
                        ptl_out = t3b[:, b_pt[su] + S:b_pt[su] + S + L].rearrange(
                            "p (t c) -> p t c", c=128)
                        Pr.dma_t("scalar", GS[su], pth_out, B3(b_ph[su], L))
                        Pr.dma_t("scalar", GS[su], ptl_out, B3(b_pl[su], L))

                    def emit_SD(i):
                        h4, qt = strips[i]
                        h = 4 * g + h4
                        su = i % 2
                        L = (qt + 1) * 128
                        Pr.wait("tensor", GS[su])
                        bav = 6 + (i % 2)
                        bank_wait("tensor", bav)
                        for jt in range(qt + 1):
                            vh_v = B3(b_vh + jt * 256 + 64 * h4, 64)
                            vl_v = B3(b_vl + jt * 256 + 64 * h4, 64)
                            pth = B3(b_pt[su] + jt * 128, 128)
                            ptl = B3(b_pt[su] + S + jt * 128, 128)
                            for i3, (wv, rv) in enumerate(((vh_v, pth), (vl_v, pth), (vh_v, ptl))):
                                Pr.emit("tensor", lambda e, w=wv, r=rv, bav=bav,
                                        f=(jt == 0 and i3 == 0), l=(jt == qt and i3 == 2):
                                        e.matmul(ps[0:64, bav, 0:128], w, r, start=f, stop=l),
                                        MM if (jt == qt and i3 == 2) else None)
                        strip_mm[su] = MM.n
                        ro = i % 2
                        Pr.wait("vector", MM)
                        Pr.wait("vector", SS[ro], ot_st[ro])
                        Pr.emit("vector", lambda e, a=V3(o_ot[ro], 128, 0, 64), bav=bav:
                                e.tensor_copy(a, ps[0:64, bav, 0:128]), EV)
                        bank_free[bav] = (EV, EV.n)
                        Pr.wait("gpsimd", EV)
                        ot_st[ro] = Pr.dma("gpsimd", SS[ro],
                                           so[64 * h:64 * h + 64, qt * 128:(qt + 1) * 128],
                                           V3(o_ot[ro], 128, 0, 64))
                        if upto == 3:
                            ot_st[ro] = Pr.dma("gpsimd", SS[ro],
                                               out[64 * h:64 * h + 64, qt * 128:(qt + 1) * 128],
                                               V3(o_ot[ro], 128, 0, 64))
                        if debug is True:
                            ot_st[ro] = Pr.dma("gpsimd", SS[ro],
                                               dbg["do"][64 * h:64 * h + 64, qt * 128:(qt + 1) * 128],
                                               V3(o_ot[ro], 128, 0, 64))

                    emit_QL(0)
                    emit_QL(1)
                    emit_QL(2)
                    for i in range(len(strips) + 2):
                        if i + 3 < len(strips):
                            emit_QL(i + 3)
                        if i < len(strips):
                            emit_SA(i)
                        if 1 <= i <= len(strips):
                            emit_SBSC(i - 1)
                        if i >= 2:
                            emit_SD(i - 2)
                    prev_group_mm[0] = MM.n
                drain_stores()
                Pr.flush(nc)

        def emit_p4():
            with nc.sbuf_tensor("p4f", [128, 43520], F32) as t4:
                a = Arena(t4, 43520)
                o_o = a.alloc(DC * P)
                o_x = a.alloc(DC * P)
                o_hs = a.alloc(DC * P)
                o_hs2 = a.alloc(DC * P)
                o_h2 = a.alloc(DC * P)
                o_mid = a.alloc(HC * P)
                o_wr = [a.alloc(1024) for _ in range(6)]

                def V4(o, n, p0=0, p1=128):
                    return t4[p0:p1, o:o + n]
                FO = lambda c: V4(o_o + c * P, P)
                FXx = lambda c: V4(o_x + c * P, P)
                FHS = lambda c: V4(o_hs + c * P, P)
                FHS2 = lambda c: V4(o_hs2 + c * P, P)
                FH2 = lambda c: V4(o_h2 + c * P, P)
                FMID = lambda c: V4(o_mid + c * P, P)
                WR4 = lambda slot, sl: V4(o_wr[slot], 1024)[:, sl]

                piece_free = []
                ev_prev = [0]
                for p in range(NP):
                    tsl = slice(p * P, (p + 1) * P)
                    lds = []
                    Pr.wait("sync", EV, ev_prev[0])
                    for c in range(DC):
                        lds.append(Pr.dma("sync", LD, FO(c), so[c * 128:(c + 1) * 128, tsl]))
                        lds.append(Pr.dma("sync", LD, FXx(c), sx1[c * 128:(c + 1) * 128, tsl]))
                        lds.append(Pr.dma("sync", LD, FHS(c), shs[c * 128:(c + 1) * 128, tsl]))
                    Pr.wait("tensor", LD, lds[-1])
                    Pr.wait("vector", LD, lds[-1])
                    Pr.wait("scalar", LD, lds[-1])
                    for i in range(4):
                        Pr.wait("vector", SS[i])
                        Pr.wait("scalar", SS[i])
                    Pr.wait("vector", MM)
                    Pr.wait("scalar", MM)

                    def ep_cp(o, b):
                        Pr.wait("vector", MM)
                        Pr.emit("vector", lambda e, a=FHS2(o), b=b, s=bcol("cp", o), hsv=FHS(o):
                                e.scalar_tensor_tensor(a, ps[:, b, :], s, hsv,
                                                       op0=ALU.add, op1=ALU.add), EV)
                        bank_free[b] = (EV, EV.n)

                    emit_lin(wgen["cp"], DC, DC, FO, WR4, ep_cp, piece_free, ring_n=6)
                    Pr.wait("scalar", EV)
                    emit_ln(FHS2, FH2, FH2, gname="ln2")
                    Pr.wait("tensor", EV)

                    def ep_cf(o, b):
                        Pr.wait("scalar", MM)
                        Pr.emit("scalar", lambda e, a=FMID(o), b=b, s=bcol("cf", o):
                                e.activation(a, ps[:, b, :], AF.Gelu_apprx_tanh,
                                             bias=s, scale=1.0), E2)
                        bank_free[b] = (E2, E2.n)

                    emit_lin(wgen["cf"], HC, DC, FH2, WR4, ep_cf, piece_free, ring_n=6)
                    Pr.wait("tensor", E2)

                    def ep_cm(o, b):
                        sslot = o % 4
                        Pr.wait("vector", MM)
                        Pr.wait("vector", SS[sslot])
                        Pr.emit("vector", lambda e, a=FH2(o), b=b, s=bcol("cm", o), hs2v=FHS2(o):
                                e.scalar_tensor_tensor(a, ps[:, b, :], s, hs2v,
                                                       op0=ALU.add, op1=ALU.add))
                        Pr.emit("vector", lambda e, a=FH2(o), xv=FXx(o):
                                e.tensor_add(a, a, xv), EV)
                        bank_free[b] = (EV, EV.n)
                        Pr.wait("gpsimd", EV)
                        tgt = sx2 if upto >= 5 else out
                        Pr.dma("gpsimd", SS[sslot], tgt[o * 128:(o + 1) * 128, tsl], FH2(o))
                        if debug is True:
                            Pr.dma("gpsimd", SS[sslot], dbg["dx2"][o * 128:(o + 1) * 128, tsl], FH2(o))

                    emit_lin(wgen["cm"], DC, HC, FMID, WR4, ep_cm, piece_free, ring_n=6)
                    ev_prev = [EV.n]
                drain_stores()
                Pr.flush(nc)

        if upto >= 1:
            emit_ffn(x0, sx1 if upto >= 2 else out)
        if upto >= 2:
            emit_p2()
        if upto >= 3:
            emit_p3()
        if upto >= 4:
            emit_p4()
        if upto >= 5:
            emit_ffn(sx2, out)

        drain_stores()
        Pr.flush(nc)

    return nc, dbg


def _slab(w):
    O, Din = w.shape
    oc, dcm = O // 128, Din // 128
    return np.ascontiguousarray(
        w.reshape(oc, 128, dcm, 128).transpose(0, 3, 2, 1).reshape(oc, 128, Din))


def _cols(v):
    O = v.shape[0]
    return np.ascontiguousarray(v.reshape(O // 128, 128).T)


def make_inputs(inp, core):
    g = lambda k: np.asarray(inp[k])
    m = {}
    m["x"] = np.ascontiguousarray(g("L0_attn_hs")[core].T)
    m["w1s"] = _slab(g("L0_ffn_w1_w")); m["w2s"] = _slab(g("L0_ffn_w2_w"))
    m["h1s"] = _slab(g("L0_ffn_h1_w")); m["h2s"] = _slab(g("L0_ffn_h2_w"))
    m["w3s"] = _slab(g("L0_ffn_w3_w"))
    m["s1c"] = _cols(g("L0_ffn_w1_s")); m["s2c"] = _cols(g("L0_ffn_w2_s"))
    m["s1hc"] = _cols(g("L0_ffn_h1_s")); m["s2hc"] = _cols(g("L0_ffn_h2_s"))
    m["s3c"] = _cols(g("L0_ffn_w3_s"))
    for nm in ("ca", "cp", "cf", "cm"):
        sub = g(f"L0_attn_{nm}_sub_w")
        m[f"{nm}_sub0"] = _slab(sub[0]); m[f"{nm}_sub1"] = _slab(sub[1])
        m[f"{nm}_Ww"] = _slab(g(f"L0_attn_{nm}_W_w"))
        m[f"{nm}_Bw"] = _slab(g(f"L0_attn_{nm}_B_w"))
        m[f"{nm}_wlw"] = np.ascontiguousarray(
            np.broadcast_to(g(f"L0_attn_{nm}_Wl_w")[0], (128, 2)))
        bsub = g(f"L0_attn_{nm}_sub_b")
        m[f"{nm}_bsub0"] = _cols(bsub[0]); m[f"{nm}_bsub1"] = _cols(bsub[1])
        m[f"{nm}_bWw"] = _cols(g(f"L0_attn_{nm}_W_b"))
        m[f"{nm}_bBw"] = _cols(g(f"L0_attn_{nm}_B_b"))
        m[f"{nm}_wlb"] = np.ascontiguousarray(
            np.broadcast_to(g(f"L0_attn_{nm}_Wl_b")[0], (128, 2)))
    m["ln1g"] = _cols(g("L0_attn_ln1_g")); m["ln1b"] = _cols(g("L0_attn_ln1_b"))
    m["ln2g"] = _cols(g("L0_attn_ln2_g")); m["ln2b"] = _cols(g("L0_attn_ln2_b"))
    m["identc"] = np.eye(128, dtype=np.float32)
    m["ones128"] = np.ones((128, 128), np.float32)
    mk = np.zeros((128, 2048), np.float32)
    for r in range(4):
        qpos = np.arange(128)[:, None] + 128 * r
        kpos = np.arange(512)[None, :]
        mk[:, r * 512:(r + 1) * 512] = np.where(kpos > qpos, -10000.0, 0.0)
    m["maskc"] = mk
    m["epsrow"] = np.full((1, 1), EPS, np.float32)
    return m


_CACHE = {}
BUILD_UPTO = 5


def kernel(**inputs):
    if "nc" not in _CACHE:
        _CACHE["nc"], _ = build(upto=BUILD_UPTO)
    nc = _CACHE["nc"]
    in_maps = [make_inputs(inputs, c) for c in range(NCORES)]
    res = run_bass_kernel_spmd(nc, in_maps, core_ids=list(range(NCORES)))
    outs = [np.ascontiguousarray(np.asarray(r["out"]).T) for r in res.results]
    return np.stack(outs, axis=0)
